# version 01
# baseline (speedup 1.0000x reference)
import numpy as np
import ml_dtypes
import contextlib
import concourse.bass as bass
import concourse.mybir as mybir
from concourse.bass_utils import run_bass_kernel_spmd

F32 = mybir.dt.float32
BF16 = mybir.dt.bfloat16
F16 = mybir.dt.float16
I32 = mybir.dt.int32
U8 = mybir.dt.uint8
AF = mybir.ActivationFunctionType
ALU = mybir.AluOpType
AXX = mybir.AxisListType.X

S = 4096
E = 2048
NH = 16
NI = 5632
TB = 512
NTT = 4
NEC = 16
NIC = 44
INV_SQRT_HD = 1.0 / float(np.sqrt(128.0))
LN_EPS = 1e-5
RMS_EPS = 1e-6

C_ONE, C_EXPSC, C_M8, C_M10, C_SGS, C_SGB, C_ZERO = range(7)
CONSTS = [1.0, INV_SQRT_HD, -8.0, -10.0, -1.0 / 16.0, 7.5 / 16.0, 0.0,
          0, 0, 0, 0, 0, 0, 0, 0, 0]

ENGS = ("sync", "gpsimd", "vector", "scalar", "tensor")


class Buf:
    __slots__ = ("name", "w", "readers", "w_small")

    def __init__(self, name):
        self.name = name
        self.w = None
        self.w_small = False
        self.readers = []


class Sem:
    __slots__ = ("name", "h", "n", "lazy")

    def __init__(self, name, h):
        self.name = name
        self.h = h
        self.n = 0
        self.lazy = False


class Sched:
    def __init__(self):
        self.ops = []
        self.sems = {}
        self.water = {e: {} for e in ENGS}
        self.bufs = {}
        self.n_instr = 0

    def gb(self, name):
        if name not in self.bufs:
            self.bufs[name] = Buf(name)
        return self.bufs[name]

    def sem_for(self, eng, dma):
        key = {"sync": "dW", "gpsimd": "dG", "vector": "eV",
               "scalar": "eA", "tensor": "eT"}[eng]
        return self.sems[key]

    def op(self, eng, emit, r=(), w=(), dma=False, lazy=False, small=False,
           tag=""):
        sem = self.sem_for(eng, dma)
        amt = 16 if dma else 1
        reads = [self.gb(x) for x in r]
        writes = [self.gb(x) for x in w]
        deps = []
        for b in reads:
            if b.w is not None:
                deps.append((b.w[0], b.w[1], b.w_small))
        for b in writes:
            if b.w is not None:
                deps.append((b.w[0], b.w[1], False))
            deps.extend((s_, v_, False) for (s_, v_) in b.readers)
        need = {}
        own = None if dma else sem
        for (s, v, force) in deps:
            if s is own and not force:
                continue
            if need.get(s, -1) < v:
                need[s] = v
        wm = self.water[eng]
        waits = []
        for s, v in sorted(need.items(), key=lambda kv: kv[0].name):
            if wm.get(s, -1) >= v:
                continue
            wm[s] = v
            waits.append((s, v))
        if lazy:
            assert not dma
            val = sem.n + 1
            sem.lazy = True
        else:
            sem.n += amt
            val = sem.n
            sem.lazy = False
        for b in writes:
            b.w = (sem, val)
            b.w_small = small
            b.readers = []
        for b in reads:
            b.readers.append((sem, val))
        self.ops.append((eng, waits, emit, None if lazy else sem, amt, tag))
        self.n_instr += 1 + len(waits)

    def claim(self, eng, emitnop, names, tag="claim"):
        self.op(eng, emitnop, r=(), w=names, tag=tag)

    def barrier(self):
        for s in self.sems.values():
            assert not s.lazy, f"lazy pending on {s.name} at barrier"
        cur = [(s, s.n) for s in self.sems.values() if s.n > 0]
        for e in ENGS:
            wm = self.water[e]
            todo = [(s, v) for (s, v) in cur if wm.get(s, -1) < v]
            for (s, v) in todo:
                wm[s] = v
            if todo:
                self.ops.append((e, todo, None, None, 0, "barrier"))
                self.n_instr += len(todo)

    def emit(self, block):
        def mk(name):
            def body(eng):
                for (e, waits, emit, sem, amt, tag) in self.ops:
                    if e != name:
                        continue
                    for (s, v) in waits:
                        eng.wait_ge(s.h, v)
                    if emit is None:
                        continue
                    ins = emit(eng)
                    if sem is not None:
                        ins.then_inc(sem.h, amt)
            return body
        block.sync(mk("sync"))
        block.gpsimd(mk("gpsimd"))
        block.vector(mk("vector"))
        block.scalar(mk("scalar"))
        block.tensor(mk("tensor"))


def build_program(nblk=8, taps=None, dbg_stop=None):
    nc = bass.Bass("TRN2", target_bir_lowering=False)
    st = contextlib.ExitStack()

    x_d = nc.dram_tensor("x", [S, E], F32, kind="ExternalInput").ap()
    wqkv_d = nc.dram_tensor("wqkv", [3 * E, E], F32, kind="ExternalInput").ap()
    wout_d = nc.dram_tensor("wout", [E, E], F32, kind="ExternalInput").ap()
    gup_d = nc.dram_tensor("gup", [2 * NI, E // 8], I32, kind="ExternalInput").ap()
    gus_d = nc.dram_tensor("gus", [16, 2 * NI], F16, kind="ExternalInput").ap()
    dnp_d = nc.dram_tensor("dnp", [E, NI // 8], I32, kind="ExternalInput").ap()
    dns_d = nc.dram_tensor("dns", [48, E], F16, kind="ExternalInput").ap()
    ident_d = nc.dram_tensor("identin", [128, 128], BF16, kind="ExternalInput").ap()
    onesc_d = nc.dram_tensor("onescin", [128, 1], BF16, kind="ExternalInput").ap()
    onesr_d = nc.dram_tensor("onesrin", [1, 128], BF16, kind="ExternalInput").ap()
    mask_d = nc.dram_tensor("maskin", [128, 2048], BF16, kind="ExternalInput").ap()
    cst_d = nc.dram_tensor("cstin", [128, 16], F32, kind="ExternalInput").ap()
    y_d = nc.dram_tensor("y", [S, E], F32, kind="ExternalOutput").ap()
    sq_d = nc.dram_tensor("scr_qkv", [3 * E, E], BF16, kind="Internal").ap()
    so_d = nc.dram_tensor("scr_out", [E, E], BF16, kind="Internal").ap()
    sgu_d = nc.dram_tensor("scr_gu", [2 * NI, E], BF16, kind="Internal").ap()
    sdn_d = nc.dram_tensor("scr_dn", [E, NI], BF16, kind="Internal").ap()

    tap_outs = []
    if taps:
        for (name, shape, dtype) in taps:
            tap_outs.append((name, nc.dram_tensor(name, shape, dtype,
                                                  kind="ExternalOutput").ap()))

    sb = lambda name, shape, dt: st.enter_context(nc.sbuf_tensor(name, shape, dt))
    A = sb("A", [128, 4 * E], F32)
    B = sb("B", [128, 4 * E], F32)
    xbf = sb("xbf", [128, E], BF16)
    xfm = sb("xfm", [128, NEC * TB], BF16)
    kf0 = sb("kf0", [128, NH * TB], BF16)
    kf1 = sb("kf1", [128, NH * TB], BF16)
    vt0 = sb("vt0", [128, NTT * E], BF16)
    vt1 = sb("vt1", [128, NTT * E], BF16)
    ar1 = sb("ar1", [128, NIC * TB], BF16)
    wb = sb("wb", [128, 8 * TB], BF16)
    mkp = sb("mkp", [128, 1024], BF16)
    mko = sb("mko", [128, 1024], BF16)
    gs = sb("gs", [128, TB], BF16)
    ident = sb("ident", [128, 128], BF16)
    onescol = sb("onescol", [128, 1], BF16)
    onesrow = sb("onesrow", [1, 128], BF16)
    cst = sb("cst", [128, 16], F32)
    stats = sb("stats", [128, 16], F32)

    qf = ar1[:, 0:NH * TB]
    es = ar1[:, NH * TB:NH * TB + 4 * TB]
    af = ar1[:, 20 * TB:36 * TB]
    bcv = ar1[:, 36 * TB:40 * TB].bitcast(F32)
    bc = bcv[:, 0:512]
    srow = bcv[:, 512:1024]
    rhl = ar1[:, 40 * TB:42 * TB]
    h_fm = ar1
    wbF = wb.bitcast(F32)

    AR1_ATTN = ["qf", "es0", "es1", "es2", "es3", "af", "bc", "srow",
                "srec", "rhi", "rlo"]

    pb = [st.enter_context(nc.psum_tensor(f"pb{i}", [128, 512], F32))
          for i in range(8)]
    pbh = [p.bitcast(BF16) for p in pb]

    sch = Sched()
    for sname in ("dW", "dG", "eV", "eA", "eT"):
        h = st.enter_context(nc.semaphore(sname))
        sch.sems[sname] = Sem(sname, h)

    OP = sch.op

    def cc(i):
        return cst[:, i:i + 1]

    nop = lambda e: e.engine_nop()

    OP("sync", lambda e: e.dma_start(out=ident[:], in_=ident_d), w=["ident"], dma=True)
    OP("sync", lambda e: e.dma_start(out=onescol[:], in_=onesc_d), w=["ones"], dma=True)
    OP("sync", lambda e: e.dma_start(out=onesrow[:], in_=onesr_d), w=["ones"], dma=True)
    OP("sync", lambda e: e.dma_start(out=mkp[:], in_=mask_d[:, 0:1024]), w=["mkp"], dma=True)
    OP("sync", lambda e: e.dma_start(out=mko[:], in_=mask_d[:, 1024:2048]), w=["mko"], dma=True)
    OP("sync", lambda e: e.dma_start(out=cst[:], in_=cst_d), w=["cst"], dma=True)

    w32 = (kf0.bitcast(F32)[:, 0:E], kf1.bitcast(F32)[:, 0:E])
    dstg = (vt0[:, 0:E], vt1[:, 0:E])

    def cast_weights(src_d, dst_d, nrows, permute):
        for c in range(nrows // 128):
            sl = c % 2
            wsrc, wdst = w32[sl], dstg[sl]
            ldb, stb = f"wld{sl}", f"wst{sl}"
            OP("sync", lambda e, wsrc=wsrc, c=c, src_d=src_d: e.dma_start(
                out=wsrc[:, :], in_=src_d[128 * c:128 * c + 128, :]),
                w=[ldb], dma=True, tag="wc_ld")
            if permute:
                src5 = wsrc.rearrange("p (b J t h) -> p b h J t",
                                      b=2, J=128, t=4, h=2)
                for b in range(2):
                    for hh in range(2):
                        for q in range(4):
                            last = (b == 1 and hh == 1 and q == 3)
                            OP("vector", lambda e, b=b, hh=hh, q=q, src5=src5,
                               wdst=wdst: e.tensor_copy(
                                   wdst[:, 1024 * b + 512 * hh + 128 * q:
                                        1024 * b + 512 * hh + 128 * (q + 1)],
                                   src5[:, b, hh, 32 * q:32 * (q + 1), :]),
                               r=[ldb], w=[stb], lazy=not last, tag="wc_perm")
            else:
                OP("vector", lambda e, wsrc=wsrc, wdst=wdst:
                   e.tensor_copy(wdst[:, :], wsrc[:, :]),
                   r=[ldb], w=[stb], tag="wc_cp")
            OP("gpsimd", lambda e, c=c, wdst=wdst, dst_d=dst_d: e.dma_start(
                out=dst_d[128 * c:128 * c + 128, :], in_=wdst[:, :]),
                r=[stb], dma=True, tag="wc_st")

    cast_weights(wqkv_d, sq_d, 3 * E, True)
    cast_weights(wout_d, so_d, E, False)
    sch.barrier()

    spk = vt0.bitcast(I32)
    ssc = vt1.bitcast(F32)
    t_sgn = ar1[:, 2048:4096]
    t_m4 = ar1[:, 4096:6144]
    t_t4 = ar1[:, 6144:8192]
    t_t6 = ar1[:, 8192:10240]
    t_vv = ar1[:, 10240:12288]
    t_dense = ar1[:, 12288:14336]
    t_hq = ar1[:, 14336:16384].bitcast(I32)
    t_ub = ar1[:, 16384:16896]
    t_nib = ar1[:, 18432:19456]

    def dequant_tile(words_ap, nwords, scales_ap, store_fn, sctag):
        nb = 4 * nwords
        ne = 8 * nwords
        ub = t_ub[:, 0:nb]
        nib = t_nib[:, 0:ne]
        lb, hb = nib[:, 0:nb], nib[:, nb:ne]
        hq = t_hq[:, 0:nb]
        sgn, m4, t4, t6, vv = (t_sgn[:, 0:ne], t_m4[:, 0:ne], t_t4[:, 0:ne],
                               t_t6[:, 0:ne], t_vv[:, 0:ne])
        dense = t_dense[:, 0:ne]
        OP("vector", lambda e: e.tensor_copy(ub[:, :], words_ap.bitcast(U8)),
           r=["dq_w"], w=["dq_ub"], tag="dq_ub")
        OP("vector", lambda e: e.tensor_scalar(hq[:, :], ub[:, :], 1.0 / 16.0,
                                               -7.5 / 16.0, ALU.mult, ALU.add),
           r=["dq_ub"], w=["dq_hq"], tag="dq_hq")
        OP("vector", lambda e: e.tensor_copy(hb[:, :], hq[:, :]),
           r=["dq_hq"], w=["dq_nib"], tag="dq_hb")
        OP("vector", lambda e: e.scalar_tensor_tensor(
            lb[:, :], hb[:, :], -16.0, ub[:, :], ALU.mult, ALU.add),
            r=["dq_nib", "dq_ub"], w=["dq_nib"], tag="dq_lb")
        OP("scalar", lambda e: e.activation(sgn[:, :], nib[:, :], AF.Sign,
                                            bias=cc(C_SGB), scale=cc(C_SGS)),
           r=["dq_nib", "cst"], w=["dq_sgn"], tag="dq_sgn")
        OP("vector", lambda e: e.scalar_tensor_tensor(
            m4[:, :], sgn[:, :], 4.0, nib[:, :], ALU.mult, ALU.add),
            r=["dq_sgn", "dq_nib"], w=["dq_m4"], tag="dq_m4")
        OP("scalar", lambda e: e.activation(t4[:, :], m4[:, :], AF.Relu,
                                            bias=cc(C_M8), scale=cc(C_ONE)),
           r=["dq_m4", "cst"], w=["dq_t4"], tag="dq_t4")
        OP("scalar", lambda e: e.activation(t6[:, :], m4[:, :], AF.Relu,
                                            bias=cc(C_M10), scale=cc(C_ONE)),
           r=["dq_m4", "cst"], w=["dq_t6"], tag="dq_t6")
        OP("vector", lambda e: e.scalar_tensor_tensor(
            vv[:, :], m4[:, :], 4.0, t4[:, :], ALU.subtract, ALU.add),
            r=["dq_m4", "dq_t4"], w=["dq_vv"], tag="dq_w1")
        OP("vector", lambda e: e.scalar_tensor_tensor(
            t4[:, :], vv[:, :], 0.5, t6[:, :], ALU.mult, ALU.add),
            r=["dq_vv", "dq_t6"], w=["dq_t4"], tag="dq_v")
        OP("vector", lambda e: e.tensor_tensor(vv[:, :], t4[:, :], sgn[:, :],
                                               ALU.mult),
           r=["dq_t4", "dq_sgn"], w=["dq_vv"], tag="dq_val")
        ng = ne // 128
        v3 = vv.rearrange("p (hh c) -> p hh c", hh=2)
        d3 = dense.rearrange("p (hh c) -> p hh c", hh=2)
        for g in range(ng):
            OP("vector", lambda e, g=g: e.tensor_scalar(
                d3[:, :, 64 * g:64 * (g + 1)], v3[:, :, 64 * g:64 * (g + 1)],
                scales_ap[:, g:g + 1], None, ALU.mult),
                r=["dq_vv", sctag], w=["dq_dense"], lazy=(g < ng - 1), tag="dq_sc")
        store_fn(dense)

    sc_gu = ssc[:, 0:16]
    sc_guh = ssc[:, 16:24].bitcast(F16)[:, 0:16]
    for c in range(2 * NI // 128):
        wsl = spk[:, 0:256] if c % 2 == 0 else spk[:, 256:512]
        OP("sync", lambda e, c=c, wsl=wsl: e.dma_start(
            out=wsl[:, :], in_=gup_d[128 * c:128 * c + 128, :]),
            w=["dq_w"], dma=True, tag="gu_ld")
        OP("sync", lambda e, c=c: e.dma_start(
            out=sc_guh[:, :], in_=gus_d[:, 128 * c:128 * c + 128],
            transpose=True), w=["dq_sch"], dma=True, tag="gu_scld")
        OP("vector", lambda e: e.tensor_copy(sc_gu[:, :], sc_guh[:, :]),
           r=["dq_sch"], w=["dq_sc_f"], small=True, tag="gu_scvt")
        side_up = c >= NI // 128
        i0 = 128 * c - (NI if side_up else 0)
        bq = i0 // 1024
        wblk = 1024 if bq < 5 else 512
        blk0 = (NI if side_up else 0) + 1024 * bq
        cm = (i0 % 1024) // 128
        for half in range(2):
            wv = wsl[:, 128 * half:128 * (half + 1)]
            ecol0 = 1024 * half

            def store_gu(dense, blk0=blk0, cm=cm, wblk=wblk, ecol0=ecol0):
                if wblk == 1024:
                    dst = sgu_d[blk0:blk0 + 1024, :].rearrange(
                        "(hh jl t) c -> jl t hh c", hh=2, jl=128, t=4)[
                            16 * cm:16 * cm + 16]
                else:
                    dst = sgu_d[blk0:blk0 + 512, :].rearrange(
                        "(hh jl t) c -> jl t hh c", hh=2, jl=64, t=4)[
                            16 * cm:16 * cm + 16]
                OP("gpsimd", lambda e, dst=dst, dense=dense: e.dma_start(
                    out=dst[:, :, :, ecol0:ecol0 + 1024], in_=dense[:, :]),
                    r=["dq_dense"], dma=True, tag="gu_st")

            dequant_tile(wv, 128, sc_gu[:, 8 * half:8 * half + 8],
                         store_gu, "dq_sc_f")

    sc_dn = ssc[:, 32:76]
    sc_dnh = ssc[:, 80:104].bitcast(F16)
    for c in range(E // 128):
        wsl = spk[:, 0:704] if c % 2 == 0 else spk[:, 1024:1728]
        OP("sync", lambda e, c=c, wsl=wsl: e.dma_start(
            out=wsl[:, :], in_=dnp_d[128 * c:128 * c + 128, :]),
            w=["dq_w"], dma=True, tag="dn_ld")
        for tt in range(3):
            OP("sync", lambda e, c=c, tt=tt: e.dma_start(
                out=sc_dnh[:, 16 * tt:16 * tt + 16],
                in_=dns_d[16 * tt:16 * tt + 16, 128 * c:128 * c + 128],
                transpose=True), w=["dq_sch"], dma=True, tag="dn_scld")
        OP("vector", lambda e: e.tensor_copy(sc_dn[:, :], sc_dnh[:, 0:44]),
           r=["dq_sch"], w=["dq_sc_f"], small=True, tag="dn_scvt")
        for bq in range(6):
            nwords = 128 if bq < 5 else 64
            wv = wsl[:, 128 * bq:128 * bq + nwords]
            icol0 = 1024 * bq

            def store_dn(dense, c=c, icol0=icol0, nwords=nwords):
                OP("gpsimd", lambda e, dense=dense: e.dma_start(
                    out=sdn_d[128 * c:128 * c + 128, icol0:icol0 + 8 * nwords],
                    in_=dense[:, :]), r=["dq_dense"], dma=True, tag="dn_st")

            dequant_tile(wv, nwords, sc_dn[:, 8 * bq:8 * bq + nwords // 16],
                         store_dn, "dq_sc_f")

    sch.barrier()

    WS = 8
    wcount = [0]

    def stream_tile(src_d, row0, col0):
        i = wcount[0] % WS
        wcount[0] += 1
        slot = wb[:, 512 * i:512 * i + 512]
        OP("sync", lambda e, slot=slot: e.dma_start(
            out=slot[:, :], in_=src_d[row0:row0 + 512, col0:col0 + 128],
            transpose=True), w=[f"wb{i}"], dma=True, tag="wstream")
        return slot, f"wb{i}"

    WBALL = [f"wb{i}" for i in range(WS)]

    def transpose_into_xfm(r, src_bf):
        x5 = src_bf.rearrange("p (b J t h) -> p b h J t", b=2, J=128, t=4, h=2)
        for ec in range(NEC):
            b_, hh, q = ec // 8, (ec % 8) // 4, ec % 4
            k = 6 + (ec % 2)
            OP("tensor", lambda e, b_=b_, hh=hh, q=q, k=k, x5=x5:
               e.transpose(pbh[k][:, 0:128],
                           x5[:, b_, hh, 32 * q:32 * (q + 1), :], ident[:]),
               r=["xbf", "ident"], w=[f"ps{k}"], tag="xT")
            OP("scalar", lambda e, ec=ec, r=r, k=k:
               e.copy(xfm[:, TB * ec + 128 * r:TB * ec + 128 * r + 128],
                      pbh[k][:, 0:128]),
               r=[f"ps{k}"], w=["xfm"], tag="xTev")

    kf = (kf0, kf1)
    vt = (vt0, vt1)

    for blk in range(nblk):
        par = blk % 2
        kfC, kfP = kf[par], kf[1 - par]
        vtC, vtP = vt[par], vt[1 - par]
        kfCt, kfPt = f"kf{par}", f"kf{1 - par}"
        vtCt, vtPt = f"vt{par}", f"vt{1 - par}"
        tb0 = blk * TB

        sch.claim("vector", nop, ["h"] + AR1_ATTN, tag="claimA")

        for r in range(NTT):
            Br = B[:, E * r:E * (r + 1)]
            OP("gpsimd", lambda e, r=r, Br=Br, tb0=tb0: e.dma_start(
                out=Br[:, :], in_=x_d[tb0 + 128 * r:tb0 + 128 * r + 128, :]),
                w=[f"B{r}"], dma=True, tag="xload")
            OP("vector", lambda e, Br=Br: e.tensor_copy(xbf[:], Br[:, :]),
               r=[f"B{r}"], w=["xbf"], tag="xcast")
            transpose_into_xfm(r, xbf)

        if dbg_stop == "p1" and blk == nblk - 1:
            break
        for fg in range(12):
            rowbase = 512 * fg
            for kc in range(NEC):
                slot, stag = stream_tile(sq_d, rowbase, 128 * kc)
                for m in range(4):
                    first, last = kc == 0, kc == NEC - 1
                    if fg < 8:
                        OP("tensor", lambda e, slot=slot, kc=kc, m=m,
                           first=first, last=last: e.matmul(
                               pb[m][:, :], slot[:, 128 * m:128 * (m + 1)],
                               xfm[:, TB * kc:TB * (kc + 1)],
                               start=first, stop=last),
                           r=[stag, "xfm"], w=[f"ps{m}"], lazy=(m < 3),
                           tag="qk_mm")
                    else:
                        OP("tensor", lambda e, slot=slot, kc=kc, m=m,
                           first=first, last=last: e.matmul(
                               pb[m][:, :],
                               xfm[:, TB * kc + 128 * m:TB * kc + 128 * (m + 1)],
                               slot[:, :], start=first, stop=last),
                           r=[stag, "xfm"], w=[f"ps{m}"], lazy=(m < 3),
                           tag="v_mm")
            for m in range(4):
                if fg < 4:
                    hch = 4 * fg + m
                    OP("scalar", lambda e, hch=hch, m=m:
                       e.copy(qf[:, TB * hch:TB * (hch + 1)], pb[m][:, :]),
                       r=[f"ps{m}"], w=["qf"], tag="q_ev")
                elif fg < 8:
                    hch = 4 * (fg - 4) + m
                    OP("scalar", lambda e, hch=hch, m=m, kfC=kfC:
                       e.copy(kfC[:, TB * hch:TB * (hch + 1)], pb[m][:, :]),
                       r=[f"ps{m}"], w=[kfCt], tag="k_ev")
                else:
                    vg = fg - 8
                    OP("scalar", lambda e, vg=vg, m=m, vtC=vtC:
                       e.copy(vtC[:, E * m + 512 * vg:E * m + 512 * (vg + 1)],
                              pb[m][:, :]),
                       r=[f"ps{m}"], w=[vtCt], tag="v_ev")

        if dbg_stop == "p2" and blk == nblk - 1:
            break
        c_lo = 4 if blk == 0 else 0
        for hh in range(NH):
            for c in range(c_lo, 8):
                k = c % 2
                if c < 4:
                    ksrc, ktag = kfP, kfPt
                    kcol = TB * hh + 128 * c
                    msrc, mt = mkp, "mkp"
                    mcol = 512 - 128 * c
                    vsrc, vtag = vtP, vtPt
                    vcol = E * c + 128 * hh
                else:
                    ksrc, ktag = kfC, kfCt
                    kcol = TB * hh + 128 * (c - 4)
                    msrc, mt = mko, "mko"
                    mcol = 512 - 128 * (c - 4)
                    vsrc, vtag = vtC, vtCt
                    vcol = E * (c - 4) + 128 * hh
                OP("tensor", lambda e, ksrc=ksrc, kcol=kcol, k=k, hh=hh:
                   e.matmul(pb[k][:, :], ksrc[:, kcol:kcol + 128],
                            qf[:, TB * hh:TB * (hh + 1)], start=True, stop=True),
                   r=[ktag, "qf"], w=[f"ps{k}"], tag="score")
                OP("scalar", lambda e, k=k: e.activation(
                    gs[:], pb[k][:, :], AF.Exp, bias=cc(C_ZERO),
                    scale=cc(C_EXPSC)),
                    r=[f"ps{k}", "cst"], w=["gs"], tag="exp")
                OP("vector", lambda e, c=c, msrc=msrc, mcol=mcol:
                   e.tensor_tensor(es[:, 512 * (c % 4):512 * (c % 4) + 512],
                                   gs[:], msrc[:, mcol:mcol + 512], ALU.mult),
                   r=["gs", mt], w=[f"es{c % 4}"], tag="mask")
                OP("tensor", lambda e, c=c, c_lo=c_lo: e.matmul(
                    pb[4][0:1, :], onescol[:, 0:1],
                    es[:, 512 * (c % 4):512 * (c % 4) + 512],
                    start=(c == c_lo), stop=(c == 7)),
                    r=[f"es{c % 4}", "ones"], w=["ps4"], lazy=(c < 7),
                    tag="summ")
                OP("tensor", lambda e, vsrc=vsrc, vcol=vcol, c=c, c_lo=c_lo:
                   e.matmul(pb[2][:, :], vsrc[:, vcol:vcol + 128],
                            es[:, 512 * (c % 4):512 * (c % 4) + 512],
                            start=(c == c_lo), stop=(c == 7)),
                   r=[vtag, f"es{c % 4}"], w=["ps2"], lazy=(c < 7), tag="pv")
            OP("scalar", lambda e: e.copy(srow[0:1, :], pb[4][0:1, :]),
               r=["ps4"], w=["srow"], small=True, tag="sum_ev")
            OP("vector", lambda e: e.reciprocal(bcv[0:1, 512:1024],
                                                srow[0:1, 0:512]),
               r=["srow"], w=["srec"], small=True, tag="recip")
            OP("vector", lambda e: e.tensor_copy(rhl[0:1, 0:512],
                                                 bcv[0:1, 512:1024]),
               r=["srec"], w=["rhi"], small=True, tag="rhi")
            OP("vector", lambda e: e.scalar_tensor_tensor(
                rhl[0:1, 512:1024], rhl[0:1, 0:512], -1.0,
                bcv[0:1, 512:1024], ALU.mult, ALU.add),
                r=["rhi", "srec"], w=["rlo"], small=True, tag="rlo")
            OP("tensor", lambda e: e.matmul(pb[5][:, :], onesrow[:, :],
                                            rhl[0:1, 0:512], start=True,
                                            stop=False),
               r=["rhi", "ones"], w=["ps5"], lazy=True, tag="bch")
            OP("tensor", lambda e: e.matmul(pb[5][:, :], onesrow[:, :],
                                            rhl[0:1, 512:1024], start=False,
                                            stop=True),
               r=["rlo", "ones"], w=["ps5"], tag="bcl")
            OP("scalar", lambda e: e.copy(bc[:, :], pb[5][:, :]),
               r=["ps5"], w=["bc"], tag="bc_ev")
            OP("vector", lambda e, hh=hh: e.scalar_tensor_tensor(
                af[:, TB * hh:TB * (hh + 1)], pb[2][:, :], 1.0, bc[:, :],
                ALU.bypass, ALU.mult),
                r=["ps2", "bc"], w=["af"], tag="attn_ev")

        if dbg_stop == "p3" and blk == nblk - 1:
            break
        for og in range(4):
            for kc in range(NEC):
                slot, stag = stream_tile(so_d, 512 * og, 128 * kc)
                for m in range(4):
                    OP("tensor", lambda e, slot=slot, kc=kc, m=m:
                       e.matmul(pb[m][:, :],
                                af[:, TB * kc + 128 * m:TB * kc + 128 * (m + 1)],
                                slot[:, :], start=(kc == 0), stop=(kc == NEC - 1)),
                       r=[stag, "af"], w=[f"ps{m}"], lazy=(m < 3), tag="pj_mm")
            for r in range(4):
                OP("vector", lambda e, r=r, og=og: e.scalar_tensor_tensor(
                    A[:, E * r + 512 * og:E * r + 512 * (og + 1)], pb[r][:, :],
                    1.0, B[:, E * r + 512 * og:E * r + 512 * (og + 1)],
                    ALU.bypass, ALU.add),
                    r=[f"ps{r}", f"B{r}"], w=[f"A{r}"], tag="resid1")
        if dbg_stop == "p4a" and blk == nblk - 1:
            break
        for r in range(NTT):
            Ar = A[:, E * r:E * (r + 1)]
            Br = B[:, E * r:E * (r + 1)]
            OP("vector", lambda e, Ar=Ar: e.reduce_sum(stats[:, 0:1], Ar[:, :],
                                                       axis=AXX),
               r=[f"A{r}"], w=["st0"], small=True, tag="l1sum")
            OP("vector", lambda e: e.tensor_scalar(stats[:, 1:2], stats[:, 0:1],
                                                   1.0 / E, None, ALU.mult),
               r=["st0"], w=["st1"], small=True, tag="l1mean")
            OP("vector", lambda e, Ar=Ar, Br=Br: e.tensor_scalar(
                Br[:, :], Ar[:, :], stats[:, 1:2], None, ALU.subtract),
                r=[f"A{r}", "st1"], w=[f"B{r}"], tag="l1ctr")
            OP("scalar", lambda e, Br=Br: e.activation(
                xbf[:], Br[:, :], AF.Square, bias=cc(C_ZERO), scale=cc(C_ONE),
                accum_out=stats[:, 2:3]),
                r=[f"B{r}", "cst"], w=["st2", "xbf"], tag="l1sq")
            OP("vector", lambda e: e.tensor_scalar(
                stats[:, 3:4], stats[:, 2:3], 1.0 / E, LN_EPS, ALU.mult,
                ALU.add), r=["st2"], w=["st3"], small=True, tag="l1var")
            OP("scalar", lambda e: e.activation(
                stats[:, 4:5], stats[:, 3:4], AF.Sqrt, bias=cc(C_ZERO),
                scale=cc(C_ONE)), r=["st3", "cst"], w=["st4"], small=True, tag="l1std")
            OP("vector", lambda e: e.reciprocal(stats[:, 5:6], stats[:, 4:5]),
               r=["st4"], w=["st5"], small=True, tag="l1rstd")
            OP("vector", lambda e, Ar=Ar, Br=Br: e.tensor_scalar(
                Ar[:, :], Br[:, :], stats[:, 5:6], None, ALU.mult),
                r=[f"B{r}", "st5"], w=[f"A{r}"], tag="l1app")
            OP("vector", lambda e, Ar=Ar: e.tensor_copy(xbf[:], Ar[:, :]),
               r=[f"A{r}"], w=["xbf"], tag="h1cast")
            transpose_into_xfm(r, xbf)

        if dbg_stop == "p5" and blk == nblk - 1:
            break
        sch.claim("vector", nop, AR1_ATTN + ["h"], tag="claimH")

        for g2 in range(NI // 512):
            for side in range(2):
                rowbase = NI * side + 512 * g2
                bks = (0, 1, 2, 3) if side == 0 else (4, 5, 6, 7)
                for kc in range(NEC):
                    slot, stag = stream_tile(sgu_d, rowbase, 128 * kc)
                    for m in range(4):
                        OP("tensor", lambda e, slot=slot, kc=kc, m=m, bks=bks:
                           e.matmul(pb[bks[m]][:, :],
                                    slot[:, 128 * m:128 * (m + 1)],
                                    xfm[:, TB * kc:TB * (kc + 1)],
                                    start=(kc == 0), stop=(kc == NEC - 1)),
                           r=[stag, "xfm"], w=[f"ps{bks[m]}"], lazy=(m < 3),
                           tag="gu_mm")
            for m in range(4):
                ic = 4 * g2 + m
                OP("scalar", lambda e, m=m: e.activation(
                    gs[:], pb[m][:, :], AF.Silu, bias=cc(C_ZERO),
                    scale=cc(C_ONE)),
                    r=[f"ps{m}", "cst"], w=["gs"], tag="silu")
                OP("vector", lambda e, ic=ic, m=m: e.tensor_tensor(
                    h_fm[:, TB * ic:TB * (ic + 1)], gs[:], pb[4 + m][:, :],
                    ALU.mult), r=["gs", f"ps{4 + m}"], w=["h"], tag="silmul")

        if dbg_stop == "p6" and blk == nblk - 1:
            break
        for og in range(4):
            for ic in range(NIC):
                slot, stag = stream_tile(sdn_d, 512 * og, 128 * ic)
                for r in range(4):
                    OP("tensor", lambda e, slot=slot, ic=ic, r=r:
                       e.matmul(pb[r][:, :],
                                h_fm[:, TB * ic + 128 * r:TB * ic + 128 * (r + 1)],
                                slot[:, :], start=(ic == 0),
                                stop=(ic == NIC - 1)),
                       r=[stag, "h"], w=[f"ps{r}"], lazy=(r < 3), tag="dn_mm")
            for r in range(4):
                OP("scalar", lambda e, r=r, og=og: e.copy(
                    B[:, E * r + 512 * og:E * r + 512 * (og + 1)], pb[r][:, :]),
                    r=[f"ps{r}"], w=[f"B{r}"], tag="y_ev")
        if dbg_stop == "p7a" and blk == nblk - 1:
            break
        for r in range(NTT):
            Ar = A[:, E * r:E * (r + 1)]
            Br = B[:, E * r:E * (r + 1)]
            Wr = wbF[:, 0:E]
            OP("scalar", lambda e, Br=Br: e.activation(
                xbf[:], Br[:, :], AF.Square, bias=cc(C_ZERO), scale=cc(C_ONE),
                accum_out=stats[:, 8:9]),
                r=[f"B{r}", "cst"], w=["st8", "xbf"], tag="rsq")
            OP("vector", lambda e: e.tensor_scalar(
                stats[:, 9:10], stats[:, 8:9], 1.0 / E, RMS_EPS, ALU.mult,
                ALU.add), r=["st8"], w=["st9"], small=True, tag="rms")
            OP("scalar", lambda e: e.activation(
                stats[:, 10:11], stats[:, 9:10], AF.Sqrt, bias=cc(C_ZERO),
                scale=cc(C_ONE)), r=["st9", "cst"], w=["st10"], small=True, tag="rsqt")
            OP("vector", lambda e: e.reciprocal(stats[:, 11:12],
                                                stats[:, 10:11]),
               r=["st10"], w=["st11"], small=True, tag="rinv")
            OP("vector", lambda e, Br=Br, Ar=Ar, Wr=Wr: e.scalar_tensor_tensor(
                Wr[:, :], Br[:, :], stats[:, 11:12], Ar[:, :], ALU.mult,
                ALU.add),
                r=[f"B{r}", "st11", f"A{r}"], w=WBALL, tag="resid2")
            OP("vector", lambda e, Wr=Wr: e.reduce_sum(stats[:, 0:1], Wr[:, :],
                                                       axis=AXX),
               r=WBALL, w=["st0"], small=True, tag="l2sum")
            OP("vector", lambda e: e.tensor_scalar(stats[:, 1:2], stats[:, 0:1],
                                                   1.0 / E, None, ALU.mult),
               r=["st0"], w=["st1"], small=True, tag="l2mean")
            OP("vector", lambda e, Ar=Ar, Wr=Wr: e.tensor_scalar(
                Ar[:, :], Wr[:, :], stats[:, 1:2], None, ALU.subtract),
                r=WBALL + ["st1"], w=[f"A{r}"], tag="l2ctr")
            OP("scalar", lambda e, Ar=Ar: e.activation(
                xbf[:], Ar[:, :], AF.Square, bias=cc(C_ZERO), scale=cc(C_ONE),
                accum_out=stats[:, 2:3]),
                r=[f"A{r}", "cst"], w=["st2", "xbf"], tag="l2sq")
            OP("vector", lambda e: e.tensor_scalar(
                stats[:, 3:4], stats[:, 2:3], 1.0 / E, LN_EPS, ALU.mult,
                ALU.add), r=["st2"], w=["st3"], small=True, tag="l2var")
            OP("scalar", lambda e: e.activation(
                stats[:, 4:5], stats[:, 3:4], AF.Sqrt, bias=cc(C_ZERO),
                scale=cc(C_ONE)), r=["st3", "cst"], w=["st4"], small=True, tag="l2std")
            OP("vector", lambda e: e.reciprocal(stats[:, 5:6], stats[:, 4:5]),
               r=["st4"], w=["st5"], small=True, tag="l2rstd")
            OP("vector", lambda e, Ar=Ar, Br=Br: e.tensor_scalar(
                Br[:, :], Ar[:, :], stats[:, 5:6], None, ALU.mult),
                r=[f"A{r}", "st5"], w=[f"B{r}"], tag="l2app")
            OP("gpsimd", lambda e, r=r, Br=Br, tb0=tb0: e.dma_start(
                out=y_d[tb0 + 128 * r:tb0 + 128 * r + 128, :], in_=Br[:, :]),
                r=[f"B{r}"], dma=True, tag="ystore")

    sch.barrier()
    tapmap = dict(A=A, B=B, xfm=xfm, qf=qf, es=es, af=af, kf0=kf0, kf1=kf1,
                  vt0=vt0, vt1=vt1, h=ar1, wb=wb, xbf=xbf, stats=stats, bc=bc,
                  sdn=sdn_d, sgu=sgu_d, sq=sq_d)
    for (name, ap_out) in tap_outs:
        key = name.split("__")[1]
        sap = tapmap[key]
        n_r = ap_out.shape[0]
        OP("gpsimd", lambda e, ap_out=ap_out, sap=sap, n_r=n_r: e.dma_start(
            out=ap_out, in_=sap[0:n_r, :]), r=[], dma=True, tag="tap")

    sch.barrier()
    print(f"[build] ops={len(sch.ops)} instrs~={sch.n_instr} "
          f"sbuf_free={nc.sbuf_bytes_remaining} psum_free={nc.psum_banks_remaining}")
    blk_cm = nc.Block()
    block = blk_cm.__enter__()
    sch.emit(block)
    blk_cm.__exit__(None, None, None)
    st.close()
    return nc


def _consts_np():
    ident = np.eye(128, dtype=ml_dtypes.bfloat16)
    onesc = np.ones((128, 1), dtype=ml_dtypes.bfloat16)
    onesr = np.ones((1, 128), dtype=ml_dtypes.bfloat16)
    r = np.arange(128)[:, None]
    u = np.arange(1024)[None, :]
    MP = (r >= (u - 512)).astype(ml_dtypes.bfloat16)
    MO = (r <= (u - 512)).astype(ml_dtypes.bfloat16)
    mask = np.concatenate([MP, MO], axis=1)
    cst = np.tile(np.array(CONSTS, dtype=np.float32), (128, 1))
    return ident, onesc, onesr, mask, cst


def make_in_map(xb, wqkv, wout, gup, gus, dnp, dns):
    ident, onesc, onesr, mask, cst = _consts_np()
    return dict(x=np.ascontiguousarray(xb), wqkv=np.asarray(wqkv),
                wout=np.asarray(wout), gup=np.asarray(gup),
                gus=np.asarray(gus), dnp=np.asarray(dnp),
                dns=np.pad(np.asarray(dns), ((0, 4), (0, 0))), identin=ident, onescin=onesc,
                onesrin=onesr, maskin=mask, cstin=cst)


def kernel(L0_attn_x, L0_attn_w_qkv, L0_attn_w_out, L0_ffn_gate_up_packed,
           L0_ffn_gate_up_scales, L0_ffn_down_packed, L0_ffn_down_scales):
    nc = build_program(nblk=8)
    in_maps = [make_in_map(L0_attn_x[b], L0_attn_w_qkv, L0_attn_w_out,
                           L0_ffn_gate_up_packed, L0_ffn_gate_up_scales,
                           L0_ffn_down_packed, L0_ffn_down_scales)
               for b in range(8)]
    res = run_bass_kernel_spmd(nc, in_maps, core_ids=list(range(8)))
    return np.stack([np.asarray(r["y"], dtype=np.float32)
                     for r in res.results])


# version 02
# speedup vs baseline: 1.0395x; 1.0395x over previous
import numpy as np
import ml_dtypes
import contextlib
import concourse.bass as bass
import concourse.mybir as mybir
from concourse.bass_utils import run_bass_kernel_spmd

F32 = mybir.dt.float32
BF16 = mybir.dt.bfloat16
F16 = mybir.dt.float16
I32 = mybir.dt.int32
U8 = mybir.dt.uint8
AF = mybir.ActivationFunctionType
ALU = mybir.AluOpType
AXX = mybir.AxisListType.X

S = 4096
E = 2048
NH = 16
NI = 5632
TB = 512
NTT = 4
NEC = 16
NIC = 44
INV_SQRT_HD = 1.0 / float(np.sqrt(128.0))
LN_EPS = 1e-5
RMS_EPS = 1e-6

C_ONE, C_EXPSC, C_M8, C_M10, C_SGS, C_SGB, C_ZERO = range(7)
CONSTS = [1.0, INV_SQRT_HD, -8.0, -10.0, -1.0 / 16.0, 7.5 / 16.0, 0.0,
          0, 0, 0, 0, 0, 0, 0, 0, 0]

ENGS = ("sync", "gpsimd", "vector", "scalar", "tensor")


class Buf:
    __slots__ = ("name", "w", "readers", "w_small")

    def __init__(self, name):
        self.name = name
        self.w = None
        self.w_small = False
        self.readers = []


class Sem:
    __slots__ = ("name", "h", "n", "lazy")

    def __init__(self, name, h):
        self.name = name
        self.h = h
        self.n = 0
        self.lazy = False


class Sched:
    def __init__(self):
        self.ops = []
        self.sems = {}
        self.water = {e: {} for e in ENGS}
        self.bufs = {}
        self.n_instr = 0

    def gb(self, name):
        if name not in self.bufs:
            self.bufs[name] = Buf(name)
        return self.bufs[name]

    def sem_for(self, eng, dma):
        key = {"sync": "dW", "gpsimd": "dG", "vector": "eV",
               "scalar": "eA", "tensor": "eT"}[eng]
        return self.sems[key]

    def op(self, eng, emit, r=(), w=(), dma=False, lazy=False, small=False,
           tag=""):
        sem = self.sem_for(eng, dma)
        amt = 16 if dma else 1
        reads = [self.gb(x) for x in r]
        writes = [self.gb(x) for x in w]
        deps = []
        for b in reads:
            if b.w is not None:
                deps.append((b.w[0], b.w[1], b.w_small))
        for b in writes:
            if b.w is not None:
                deps.append((b.w[0], b.w[1], False))
            deps.extend((s_, v_, False) for (s_, v_) in b.readers)
        need = {}
        own = None if dma else sem
        for (s, v, force) in deps:
            if s is own and not force:
                continue
            if need.get(s, -1) < v:
                need[s] = v
        wm = self.water[eng]
        waits = []
        for s, v in sorted(need.items(), key=lambda kv: kv[0].name):
            if wm.get(s, -1) >= v:
                continue
            wm[s] = v
            waits.append((s, v))
        if lazy:
            assert not dma
            val = sem.n + 1
            sem.lazy = True
        else:
            sem.n += amt
            val = sem.n
            sem.lazy = False
        for b in writes:
            b.w = (sem, val)
            b.w_small = small
            b.readers = []
        for b in reads:
            b.readers.append((sem, val))
        self.ops.append((eng, waits, emit, None if lazy else sem, amt, tag))
        self.n_instr += 1 + len(waits)

    def claim(self, eng, emitnop, names, tag="claim"):
        self.op(eng, emitnop, r=(), w=names, tag=tag)

    def barrier(self):
        for s in self.sems.values():
            assert not s.lazy, f"lazy pending on {s.name} at barrier"
        cur = [(s, s.n) for s in self.sems.values() if s.n > 0]
        for e in ENGS:
            wm = self.water[e]
            todo = [(s, v) for (s, v) in cur if wm.get(s, -1) < v]
            for (s, v) in todo:
                wm[s] = v
            if todo:
                self.ops.append((e, todo, None, None, 0, "barrier"))
                self.n_instr += len(todo)

    def emit(self, block):
        def mk(name):
            def body(eng):
                for (e, waits, emit, sem, amt, tag) in self.ops:
                    if e != name:
                        continue
                    for (s, v) in waits:
                        eng.wait_ge(s.h, v)
                    if emit is None:
                        continue
                    ins = emit(eng)
                    if sem is not None:
                        ins.then_inc(sem.h, amt)
            return body
        block.sync(mk("sync"))
        block.gpsimd(mk("gpsimd"))
        block.vector(mk("vector"))
        block.scalar(mk("scalar"))
        block.tensor(mk("tensor"))


def build_program(nblk=8, taps=None, dbg_stop=None):
    nc = bass.Bass("TRN2", target_bir_lowering=False)
    st = contextlib.ExitStack()

    x_d = nc.dram_tensor("x", [S, E], F32, kind="ExternalInput").ap()
    wqkv_d = nc.dram_tensor("wqkv", [3 * E, E], F32, kind="ExternalInput").ap()
    wout_d = nc.dram_tensor("wout", [E, E], F32, kind="ExternalInput").ap()
    gup_d = nc.dram_tensor("gup", [2 * NI, E // 8], I32, kind="ExternalInput").ap()
    gus_d = nc.dram_tensor("gus", [16, 2 * NI], F16, kind="ExternalInput").ap()
    dnp_d = nc.dram_tensor("dnp", [E, NI // 8], I32, kind="ExternalInput").ap()
    dns_d = nc.dram_tensor("dns", [48, E], F16, kind="ExternalInput").ap()
    ident_d = nc.dram_tensor("identin", [128, 128], BF16, kind="ExternalInput").ap()
    onesc_d = nc.dram_tensor("onescin", [128, 1], BF16, kind="ExternalInput").ap()
    onesr_d = nc.dram_tensor("onesrin", [1, 128], BF16, kind="ExternalInput").ap()
    mask_d = nc.dram_tensor("maskin", [128, 2048], BF16, kind="ExternalInput").ap()
    cst_d = nc.dram_tensor("cstin", [128, 16], F32, kind="ExternalInput").ap()
    y_d = nc.dram_tensor("y", [S, E], F32, kind="ExternalOutput").ap()
    sqT_d = nc.dram_tensor("scr_qkvT", [E, 3 * E], BF16, kind="Internal").ap()
    soT_d = nc.dram_tensor("scr_outT", [E, E], BF16, kind="Internal").ap()
    sguT_d = nc.dram_tensor("scr_guT", [E, 2 * NI], BF16, kind="Internal").ap()
    sdnT_d = nc.dram_tensor("scr_dnT", [NI, E], BF16, kind="Internal").ap()

    tap_outs = []
    if taps:
        for (name, shape, dtype) in taps:
            tap_outs.append((name, nc.dram_tensor(name, shape, dtype,
                                                  kind="ExternalOutput").ap()))

    sb = lambda name, shape, dt: st.enter_context(nc.sbuf_tensor(name, shape, dt))
    A = sb("A", [128, 4 * E], F32)
    B = sb("B", [128, 4 * E], F32)
    xbf = sb("xbf", [128, E], BF16)
    xfm = sb("xfm", [128, NEC * TB], BF16)
    kf0 = sb("kf0", [128, NH * TB], BF16)
    kf1 = sb("kf1", [128, NH * TB], BF16)
    vt0 = sb("vt0", [128, NTT * E], BF16)
    vt1 = sb("vt1", [128, NTT * E], BF16)
    ar1 = sb("ar1", [128, NIC * TB], BF16)
    wb = sb("wb", [128, 8 * TB], BF16)
    mkp = sb("mkp", [128, 1024], BF16)
    mko = sb("mko", [128, 1024], BF16)
    gs = sb("gs", [128, TB], BF16)
    ident = sb("ident", [128, 128], BF16)
    onescol = sb("onescol", [128, 1], BF16)
    onesrow = sb("onesrow", [1, 128], BF16)
    cst = sb("cst", [128, 16], F32)
    stats = sb("stats", [128, 16], F32)

    qf = ar1[:, 0:NH * TB]
    es = ar1[:, NH * TB:NH * TB + 4 * TB]
    af = ar1[:, 20 * TB:36 * TB]
    bcv = ar1[:, 36 * TB:40 * TB].bitcast(F32)
    bc = bcv[:, 0:512]
    srow = bcv[:, 512:1024]
    rhl = ar1[:, 40 * TB:42 * TB]
    h_fm = ar1
    wbF = wb.bitcast(F32)

    AR1_ATTN = ["qf", "es0", "es1", "es2", "es3", "af", "bc", "srow",
                "srec", "rhi", "rlo"]

    pb = [st.enter_context(nc.psum_tensor(f"pb{i}", [128, 512], F32))
          for i in range(8)]
    pbh = [p.bitcast(BF16) for p in pb]

    sch = Sched()
    for sname in ("dW", "dG", "eV", "eA", "eT"):
        h = st.enter_context(nc.semaphore(sname))
        sch.sems[sname] = Sem(sname, h)

    OP = sch.op

    def cc(i):
        return cst[:, i:i + 1]

    nop = lambda e: e.engine_nop()

    OP("sync", lambda e: e.dma_start(out=ident[:], in_=ident_d), w=["ident"], dma=True)
    OP("sync", lambda e: e.dma_start(out=onescol[:], in_=onesc_d), w=["ones"], dma=True)
    OP("sync", lambda e: e.dma_start(out=onesrow[:], in_=onesr_d), w=["ones"], dma=True)
    OP("sync", lambda e: e.dma_start(out=mkp[:], in_=mask_d[:, 0:1024]), w=["mkp"], dma=True)
    OP("sync", lambda e: e.dma_start(out=mko[:], in_=mask_d[:, 1024:2048]), w=["mko"], dma=True)
    OP("sync", lambda e: e.dma_start(out=cst[:], in_=cst_d), w=["cst"], dma=True)

    stg = xfm

    def t_stage(dense, dense_tag, ec0, n_ec, osub, tag):
        for ecl in range(n_ec):
            ec = ec0 + ecl
            k = 6 + (ec % 2)
            OP("tensor", lambda e, ecl=ecl, k=k, dense=dense:
               e.transpose(pbh[k][:, 0:128],
                           dense[:, 128 * ecl:128 * (ecl + 1)], ident[:]),
               r=[dense_tag, "ident"], w=[f"ps{k}"], tag=tag + "T")
            OP("scalar", lambda e, ec=ec, osub=osub, k=k:
               e.copy(stg[:, 512 * ec + 128 * osub:512 * ec + 128 * osub + 128],
                      pbh[k][:, 0:128]),
               r=[f"ps{k}"], w=[f"stg{ec}"], tag=tag + "Tev")

    def t_flush(dstT, ocol0, tag):
        for ec in range(16):
            OP("gpsimd", lambda e, ec=ec, dstT=dstT, ocol0=ocol0: e.dma_start(
                out=dstT[128 * ec:128 * ec + 128, ocol0:ocol0 + 512],
                in_=stg[:, 512 * ec:512 * ec + 512]),
                r=[f"stg{ec}"], dma=True, tag=tag + "st")

    w32 = (kf0.bitcast(F32)[:, 0:E], kf1.bitcast(F32)[:, 0:E])
    dstg = (vt0[:, 0:E], vt1[:, 0:E])

    def cast_weights(src_d, dstT, nrows):
        for c in range(nrows // 128):
            sl = c % 2
            wsrc, wdst = w32[sl], dstg[sl]
            ldb, stb = f"wld{sl}", f"wst{sl}"
            OP("sync", lambda e, wsrc=wsrc, c=c, src_d=src_d: e.dma_start(
                out=wsrc[:, :], in_=src_d[128 * c:128 * c + 128, :]),
                w=[ldb], dma=True, tag="wc_ld")
            OP("vector", lambda e, wsrc=wsrc, wdst=wdst:
               e.tensor_copy(wdst[:, :], wsrc[:, :]),
               r=[ldb], w=[stb], tag="wc_cp")
            t_stage(wdst, stb, 0, 16, c % 4, "wc")
            if c % 4 == 3:
                t_flush(dstT, 512 * (c // 4), "wc")

    cast_weights(wqkv_d, sqT_d, 3 * E)
    cast_weights(wout_d, soT_d, E)
    sch.barrier()

    spk = vt0.bitcast(I32)
    ssc = vt1.bitcast(F32)
    t_sgn = ar1[:, 2048:4096]
    t_m4 = ar1[:, 4096:6144]
    t_t4 = ar1[:, 6144:8192]
    t_t6 = ar1[:, 8192:10240]
    t_vv = ar1[:, 10240:12288]
    t_dense = ar1[:, 12288:14336]
    t_hq = ar1[:, 14336:16384].bitcast(I32)
    t_ub = ar1[:, 16384:16896]
    t_nib = ar1[:, 18432:19456]

    def dequant_tile(words_ap, nwords, scales_ap, store_fn, sctag):
        nb = 4 * nwords
        ne = 8 * nwords
        ub = t_ub[:, 0:nb]
        nib = t_nib[:, 0:ne]
        lb, hb = nib[:, 0:nb], nib[:, nb:ne]
        hq = t_hq[:, 0:nb]
        sgn, m4, t4, t6, vv = (t_sgn[:, 0:ne], t_m4[:, 0:ne], t_t4[:, 0:ne],
                               t_t6[:, 0:ne], t_vv[:, 0:ne])
        dense = t_dense[:, 0:ne]
        OP("vector", lambda e: e.tensor_copy(ub[:, :], words_ap.bitcast(U8)),
           r=["dq_w"], w=["dq_ub"], tag="dq_ub")
        OP("vector", lambda e: e.tensor_scalar(hq[:, :], ub[:, :], 1.0 / 16.0,
                                               -7.5 / 16.0, ALU.mult, ALU.add),
           r=["dq_ub"], w=["dq_hq"], tag="dq_hq")
        OP("vector", lambda e: e.tensor_copy(hb[:, :], hq[:, :]),
           r=["dq_hq"], w=["dq_nib"], tag="dq_hb")
        OP("vector", lambda e: e.scalar_tensor_tensor(
            lb[:, :], hb[:, :], -16.0, ub[:, :], ALU.mult, ALU.add),
            r=["dq_nib", "dq_ub"], w=["dq_nib"], tag="dq_lb")
        OP("scalar", lambda e: e.activation(sgn[:, :], nib[:, :], AF.Sign,
                                            bias=cc(C_SGB), scale=cc(C_SGS)),
           r=["dq_nib", "cst"], w=["dq_sgn"], tag="dq_sgn")
        OP("vector", lambda e: e.scalar_tensor_tensor(
            m4[:, :], sgn[:, :], 4.0, nib[:, :], ALU.mult, ALU.add),
            r=["dq_sgn", "dq_nib"], w=["dq_m4"], tag="dq_m4")
        OP("scalar", lambda e: e.activation(t4[:, :], m4[:, :], AF.Relu,
                                            bias=cc(C_M8), scale=cc(C_ONE)),
           r=["dq_m4", "cst"], w=["dq_t4"], tag="dq_t4")
        OP("scalar", lambda e: e.activation(t6[:, :], m4[:, :], AF.Relu,
                                            bias=cc(C_M10), scale=cc(C_ONE)),
           r=["dq_m4", "cst"], w=["dq_t6"], tag="dq_t6")
        OP("vector", lambda e: e.scalar_tensor_tensor(
            vv[:, :], m4[:, :], 4.0, t4[:, :], ALU.subtract, ALU.add),
            r=["dq_m4", "dq_t4"], w=["dq_vv"], tag="dq_w1")
        OP("vector", lambda e: e.scalar_tensor_tensor(
            t4[:, :], vv[:, :], 0.5, t6[:, :], ALU.mult, ALU.add),
            r=["dq_vv", "dq_t6"], w=["dq_t4"], tag="dq_v")
        OP("vector", lambda e: e.tensor_tensor(vv[:, :], t4[:, :], sgn[:, :],
                                               ALU.mult),
           r=["dq_t4", "dq_sgn"], w=["dq_vv"], tag="dq_val")
        ng = ne // 128
        v3 = vv.rearrange("p (hh c) -> p hh c", hh=2)
        d3 = dense.rearrange("p (hh c) -> p hh c", hh=2)
        for g in range(ng):
            OP("vector", lambda e, g=g: e.tensor_scalar(
                d3[:, :, 64 * g:64 * (g + 1)], v3[:, :, 64 * g:64 * (g + 1)],
                scales_ap[:, g:g + 1], None, ALU.mult),
                r=["dq_vv", sctag], w=["dq_dense"], lazy=(g < ng - 1), tag="dq_sc")
        store_fn(dense)

    sc_gu = ssc[:, 0:16]
    sc_guh = ssc[:, 16:24].bitcast(F16)[:, 0:16]
    for c in range(2 * NI // 128):
        wsl = spk[:, 0:256] if c % 2 == 0 else spk[:, 256:512]
        OP("sync", lambda e, c=c, wsl=wsl: e.dma_start(
            out=wsl[:, :], in_=gup_d[128 * c:128 * c + 128, :]),
            w=["dq_w"], dma=True, tag="gu_ld")
        OP("sync", lambda e, c=c: e.dma_start(
            out=sc_guh[:, :], in_=gus_d[:, 128 * c:128 * c + 128],
            transpose=True), w=["dq_sch"], dma=True, tag="gu_scld")
        OP("vector", lambda e: e.tensor_copy(sc_gu[:, :], sc_guh[:, :]),
           r=["dq_sch"], w=["dq_sc_f"], small=True, tag="gu_scvt")
        for half in range(2):
            wv = wsl[:, 128 * half:128 * (half + 1)]

            def store_gu(dense, c=c, half=half):
                t_stage(dense, "dq_dense", 8 * half, 8, c % 4, "gu")
                if c % 4 == 3 and half == 1:
                    t_flush(sguT_d, 512 * (c // 4), "gu")

            dequant_tile(wv, 128, sc_gu[:, 8 * half:8 * half + 8],
                         store_gu, "dq_sc_f")

    sc_dn = ssc[:, 32:76]
    sc_dnh = ssc[:, 80:104].bitcast(F16)
    for c in range(E // 128):
        wsl = spk[:, 0:704] if c % 2 == 0 else spk[:, 1024:1728]
        OP("sync", lambda e, c=c, wsl=wsl: e.dma_start(
            out=wsl[:, :], in_=dnp_d[128 * c:128 * c + 128, :]),
            w=["dq_w"], dma=True, tag="dn_ld")
        for tt in range(3):
            OP("sync", lambda e, c=c, tt=tt: e.dma_start(
                out=sc_dnh[:, 16 * tt:16 * tt + 16],
                in_=dns_d[16 * tt:16 * tt + 16, 128 * c:128 * c + 128],
                transpose=True), w=["dq_sch"], dma=True, tag="dn_scld")
        OP("vector", lambda e: e.tensor_copy(sc_dn[:, :], sc_dnh[:, 0:44]),
           r=["dq_sch"], w=["dq_sc_f"], small=True, tag="dn_scvt")
        for bq in range(6):
            nwords = 128 if bq < 5 else 64
            wv = wsl[:, 128 * bq:128 * bq + nwords]

            def store_dn(dense, c=c, bq=bq, nwords=nwords):
                for ecl in range(nwords // 16):
                    icg = 8 * bq + ecl
                    sd = kf0 if icg < 32 else kf1
                    scol = 256 * (icg % 32) + 128 * (c % 2)
                    k = 6 + (icg % 2)
                    OP("tensor", lambda e, ecl=ecl, k=k, dense=dense:
                       e.transpose(pbh[k][:, 0:128],
                                   dense[:, 128 * ecl:128 * (ecl + 1)],
                                   ident[:]),
                       r=["dq_dense", "ident"], w=[f"ps{k}"], tag="dnT")
                    OP("scalar", lambda e, sd=sd, scol=scol, k=k:
                       e.copy(sd[:, scol:scol + 128], pbh[k][:, 0:128]),
                       r=[f"ps{k}"], w=[f"std{icg}"], tag="dnTev")
                if c % 2 == 1:
                    for icg in range(8 * bq, 8 * bq + nwords // 16):
                        sd = kf0 if icg < 32 else kf1
                        OP("gpsimd", lambda e, icg=icg, sd=sd, c=c:
                           e.dma_start(
                               out=sdnT_d[128 * icg:128 * icg + 128,
                                          128 * (c - 1):128 * (c - 1) + 256],
                               in_=sd[:, 256 * (icg % 32):256 * (icg % 32) + 256]),
                           r=[f"std{icg}"], dma=True, tag="dn_st")

            dequant_tile(wv, nwords, sc_dn[:, 8 * bq:8 * bq + nwords // 16],
                         store_dn, "dq_sc_f")

    sch.barrier()

    WS = 8
    wcount = [0]

    def stream_tile(srcT_d, kc, ocol0):
        i = wcount[0] % WS
        wcount[0] += 1
        slot = wb[:, 512 * i:512 * i + 512]
        OP("sync", lambda e, slot=slot: e.dma_start(
            out=slot[:, :], in_=srcT_d[128 * kc:128 * kc + 128,
                                       ocol0:ocol0 + 512]),
            w=[f"wb{i}"], dma=True, tag="wstream")
        return slot, f"wb{i}"

    WBALL = [f"wb{i}" for i in range(WS)]

    def transpose_into_xfm(r, src_bf):
        x5 = src_bf.rearrange("p (b J t h) -> p b h J t", b=2, J=128, t=4, h=2)
        for ec in range(NEC):
            b_, hh, q = ec // 8, (ec % 8) // 4, ec % 4
            k = 6 + (ec % 2)
            OP("tensor", lambda e, b_=b_, hh=hh, q=q, k=k, x5=x5:
               e.transpose(pbh[k][:, 0:128],
                           x5[:, b_, hh, 32 * q:32 * (q + 1), :], ident[:]),
               r=["xbf", "ident"], w=[f"ps{k}"], tag="xT")
            OP("scalar", lambda e, ec=ec, r=r, k=k:
               e.copy(xfm[:, TB * ec + 128 * r:TB * ec + 128 * r + 128],
                      pbh[k][:, 0:128]),
               r=[f"ps{k}"], w=["xfm"], tag="xTev")

    kf = (kf0, kf1)
    vt = (vt0, vt1)

    for blk in range(nblk):
        par = blk % 2
        kfC, kfP = kf[par], kf[1 - par]
        vtC, vtP = vt[par], vt[1 - par]
        kfCt, kfPt = f"kf{par}", f"kf{1 - par}"
        vtCt, vtPt = f"vt{par}", f"vt{1 - par}"
        tb0 = blk * TB

        sch.claim("vector", nop, ["h"] + AR1_ATTN, tag="claimA")

        for r in range(NTT):
            Br = B[:, E * r:E * (r + 1)]
            OP("gpsimd", lambda e, r=r, Br=Br, tb0=tb0: e.dma_start(
                out=Br[:, :], in_=x_d[tb0 + 128 * r:tb0 + 128 * r + 128, :]),
                w=[f"B{r}"], dma=True, tag="xload")
            OP("vector", lambda e, Br=Br: e.tensor_copy(xbf[:], Br[:, :]),
               r=[f"B{r}"], w=["xbf"], tag="xcast")
            transpose_into_xfm(r, xbf)

        if dbg_stop == "p1" and blk == nblk - 1:
            break
        for fg in range(12):
            for kc in range(NEC):
                slot, stag = stream_tile(sqT_d, kc, 512 * fg)
                for m in range(4):
                    first, last = kc == 0, kc == NEC - 1
                    if fg < 8:
                        OP("tensor", lambda e, slot=slot, kc=kc, m=m,
                           first=first, last=last: e.matmul(
                               pb[m][:, :], slot[:, 128 * m:128 * (m + 1)],
                               xfm[:, TB * kc:TB * (kc + 1)],
                               start=first, stop=last),
                           r=[stag, "xfm"], w=[f"ps{m}"], lazy=(m < 3),
                           tag="qk_mm")
                    else:
                        OP("tensor", lambda e, slot=slot, kc=kc, m=m,
                           first=first, last=last: e.matmul(
                               pb[m][:, :],
                               xfm[:, TB * kc + 128 * m:TB * kc + 128 * (m + 1)],
                               slot[:, :], start=first, stop=last),
                           r=[stag, "xfm"], w=[f"ps{m}"], lazy=(m < 3),
                           tag="v_mm")
            for m in range(4):
                if fg < 4:
                    hch = 4 * fg + m
                    OP("scalar", lambda e, hch=hch, m=m:
                       e.copy(qf[:, TB * hch:TB * (hch + 1)], pb[m][:, :]),
                       r=[f"ps{m}"], w=["qf"], tag="q_ev")
                elif fg < 8:
                    hch = 4 * (fg - 4) + m
                    OP("scalar", lambda e, hch=hch, m=m, kfC=kfC:
                       e.copy(kfC[:, TB * hch:TB * (hch + 1)], pb[m][:, :]),
                       r=[f"ps{m}"], w=[kfCt], tag="k_ev")
                else:
                    vg = fg - 8
                    OP("scalar", lambda e, vg=vg, m=m, vtC=vtC:
                       e.copy(vtC[:, E * m + 512 * vg:E * m + 512 * (vg + 1)],
                              pb[m][:, :]),
                       r=[f"ps{m}"], w=[vtCt], tag="v_ev")

        if dbg_stop == "p2" and blk == nblk - 1:
            break
        c_lo = 4 if blk == 0 else 0
        for hh in range(NH):
            for c in range(c_lo, 8):
                k = c % 2
                if c < 4:
                    ksrc, ktag = kfP, kfPt
                    kcol = TB * hh + 128 * c
                    msrc, mt = mkp, "mkp"
                    mcol = 512 - 128 * c
                    vsrc, vtag = vtP, vtPt
                    vcol = E * c + 128 * hh
                else:
                    ksrc, ktag = kfC, kfCt
                    kcol = TB * hh + 128 * (c - 4)
                    msrc, mt = mko, "mko"
                    mcol = 512 - 128 * (c - 4)
                    vsrc, vtag = vtC, vtCt
                    vcol = E * (c - 4) + 128 * hh
                OP("tensor", lambda e, ksrc=ksrc, kcol=kcol, k=k, hh=hh:
                   e.matmul(pb[k][:, :], ksrc[:, kcol:kcol + 128],
                            qf[:, TB * hh:TB * (hh + 1)], start=True, stop=True),
                   r=[ktag, "qf"], w=[f"ps{k}"], tag="score")
                OP("scalar", lambda e, k=k: e.activation(
                    gs[:], pb[k][:, :], AF.Exp, bias=cc(C_ZERO),
                    scale=cc(C_EXPSC)),
                    r=[f"ps{k}", "cst"], w=["gs"], tag="exp")
                OP("vector", lambda e, c=c, msrc=msrc, mcol=mcol:
                   e.tensor_tensor(es[:, 512 * (c % 4):512 * (c % 4) + 512],
                                   gs[:], msrc[:, mcol:mcol + 512], ALU.mult),
                   r=["gs", mt], w=[f"es{c % 4}"], tag="mask")
                OP("tensor", lambda e, c=c, c_lo=c_lo: e.matmul(
                    pb[4][0:1, :], onescol[:, 0:1],
                    es[:, 512 * (c % 4):512 * (c % 4) + 512],
                    start=(c == c_lo), stop=(c == 7)),
                    r=[f"es{c % 4}", "ones"], w=["ps4"], lazy=(c < 7),
                    tag="summ")
                OP("tensor", lambda e, vsrc=vsrc, vcol=vcol, c=c, c_lo=c_lo:
                   e.matmul(pb[2][:, :], vsrc[:, vcol:vcol + 128],
                            es[:, 512 * (c % 4):512 * (c % 4) + 512],
                            start=(c == c_lo), stop=(c == 7)),
                   r=[vtag, f"es{c % 4}"], w=["ps2"], lazy=(c < 7), tag="pv")
            OP("scalar", lambda e: e.copy(srow[0:1, :], pb[4][0:1, :]),
               r=["ps4"], w=["srow"], small=True, tag="sum_ev")
            OP("vector", lambda e: e.reciprocal(bcv[0:1, 512:1024],
                                                srow[0:1, 0:512]),
               r=["srow"], w=["srec"], small=True, tag="recip")
            OP("vector", lambda e: e.tensor_copy(rhl[0:1, 0:512],
                                                 bcv[0:1, 512:1024]),
               r=["srec"], w=["rhi"], small=True, tag="rhi")
            OP("vector", lambda e: e.scalar_tensor_tensor(
                rhl[0:1, 512:1024], rhl[0:1, 0:512], -1.0,
                bcv[0:1, 512:1024], ALU.mult, ALU.add),
                r=["rhi", "srec"], w=["rlo"], small=True, tag="rlo")
            OP("tensor", lambda e: e.matmul(pb[5][:, :], onesrow[:, :],
                                            rhl[0:1, 0:512], start=True,
                                            stop=False),
               r=["rhi", "ones"], w=["ps5"], lazy=True, tag="bch")
            OP("tensor", lambda e: e.matmul(pb[5][:, :], onesrow[:, :],
                                            rhl[0:1, 512:1024], start=False,
                                            stop=True),
               r=["rlo", "ones"], w=["ps5"], tag="bcl")
            OP("scalar", lambda e: e.copy(bc[:, :], pb[5][:, :]),
               r=["ps5"], w=["bc"], tag="bc_ev")
            OP("vector", lambda e, hh=hh: e.scalar_tensor_tensor(
                af[:, TB * hh:TB * (hh + 1)], pb[2][:, :], 1.0, bc[:, :],
                ALU.bypass, ALU.mult),
                r=["ps2", "bc"], w=["af"], tag="attn_ev")

        if dbg_stop == "p3" and blk == nblk - 1:
            break
        for og in range(4):
            for kc in range(NEC):
                slot, stag = stream_tile(soT_d, kc, 512 * og)
                for m in range(4):
                    OP("tensor", lambda e, slot=slot, kc=kc, m=m:
                       e.matmul(pb[m][:, :],
                                af[:, TB * kc + 128 * m:TB * kc + 128 * (m + 1)],
                                slot[:, :], start=(kc == 0), stop=(kc == NEC - 1)),
                       r=[stag, "af"], w=[f"ps{m}"], lazy=(m < 3), tag="pj_mm")
            for r in range(4):
                OP("vector", lambda e, r=r, og=og: e.scalar_tensor_tensor(
                    A[:, E * r + 512 * og:E * r + 512 * (og + 1)], pb[r][:, :],
                    1.0, B[:, E * r + 512 * og:E * r + 512 * (og + 1)],
                    ALU.bypass, ALU.add),
                    r=[f"ps{r}", f"B{r}"], w=[f"A{r}"], tag="resid1")
        if dbg_stop == "p4a" and blk == nblk - 1:
            break
        for r in range(NTT):
            Ar = A[:, E * r:E * (r + 1)]
            Br = B[:, E * r:E * (r + 1)]
            OP("vector", lambda e, Ar=Ar: e.reduce_sum(stats[:, 0:1], Ar[:, :],
                                                       axis=AXX),
               r=[f"A{r}"], w=["st0"], small=True, tag="l1sum")
            OP("vector", lambda e: e.tensor_scalar(stats[:, 1:2], stats[:, 0:1],
                                                   1.0 / E, None, ALU.mult),
               r=["st0"], w=["st1"], small=True, tag="l1mean")
            OP("vector", lambda e, Ar=Ar, Br=Br: e.tensor_scalar(
                Br[:, :], Ar[:, :], stats[:, 1:2], None, ALU.subtract),
                r=[f"A{r}", "st1"], w=[f"B{r}"], tag="l1ctr")
            OP("scalar", lambda e, Br=Br: e.activation(
                xbf[:], Br[:, :], AF.Square, bias=cc(C_ZERO), scale=cc(C_ONE),
                accum_out=stats[:, 2:3]),
                r=[f"B{r}", "cst"], w=["st2", "xbf"], tag="l1sq")
            OP("vector", lambda e: e.tensor_scalar(
                stats[:, 3:4], stats[:, 2:3], 1.0 / E, LN_EPS, ALU.mult,
                ALU.add), r=["st2"], w=["st3"], small=True, tag="l1var")
            OP("scalar", lambda e: e.activation(
                stats[:, 4:5], stats[:, 3:4], AF.Sqrt, bias=cc(C_ZERO),
                scale=cc(C_ONE)), r=["st3", "cst"], w=["st4"], small=True, tag="l1std")
            OP("vector", lambda e: e.reciprocal(stats[:, 5:6], stats[:, 4:5]),
               r=["st4"], w=["st5"], small=True, tag="l1rstd")
            OP("vector", lambda e, Ar=Ar, Br=Br: e.tensor_scalar(
                Ar[:, :], Br[:, :], stats[:, 5:6], None, ALU.mult),
                r=[f"B{r}", "st5"], w=[f"A{r}"], tag="l1app")
            OP("vector", lambda e, Ar=Ar: e.tensor_copy(xbf[:], Ar[:, :]),
               r=[f"A{r}"], w=["xbf"], tag="h1cast")
            transpose_into_xfm(r, xbf)

        if dbg_stop == "p5" and blk == nblk - 1:
            break
        sch.claim("vector", nop, AR1_ATTN + ["h"], tag="claimH")

        for g2 in range(NI // 512):
            for side in range(2):
                bks = (0, 1, 2, 3) if side == 0 else (4, 5, 6, 7)
                for kc in range(NEC):
                    slot, stag = stream_tile(sguT_d, kc, NI * side + 512 * g2)
                    for m in range(4):
                        OP("tensor", lambda e, slot=slot, kc=kc, m=m, bks=bks:
                           e.matmul(pb[bks[m]][:, :],
                                    slot[:, 128 * m:128 * (m + 1)],
                                    xfm[:, TB * kc:TB * (kc + 1)],
                                    start=(kc == 0), stop=(kc == NEC - 1)),
                           r=[stag, "xfm"], w=[f"ps{bks[m]}"], lazy=(m < 3),
                           tag="gu_mm")
            for m in range(4):
                ic = 4 * g2 + m
                OP("scalar", lambda e, m=m: e.activation(
                    gs[:], pb[m][:, :], AF.Silu, bias=cc(C_ZERO),
                    scale=cc(C_ONE)),
                    r=[f"ps{m}", "cst"], w=["gs"], tag="silu")
                OP("vector", lambda e, ic=ic, m=m: e.tensor_tensor(
                    h_fm[:, TB * ic:TB * (ic + 1)], gs[:], pb[4 + m][:, :],
                    ALU.mult), r=["gs", f"ps{4 + m}"], w=["h"], tag="silmul")

        if dbg_stop == "p6" and blk == nblk - 1:
            break
        for og in range(4):
            for ic in range(NIC):
                slot, stag = stream_tile(sdnT_d, ic, 512 * og)
                for r in range(4):
                    OP("tensor", lambda e, slot=slot, ic=ic, r=r:
                       e.matmul(pb[r][:, :],
                                h_fm[:, TB * ic + 128 * r:TB * ic + 128 * (r + 1)],
                                slot[:, :], start=(ic == 0),
                                stop=(ic == NIC - 1)),
                       r=[stag, "h"], w=[f"ps{r}"], lazy=(r < 3), tag="dn_mm")
            for r in range(4):
                OP("scalar", lambda e, r=r, og=og: e.copy(
                    B[:, E * r + 512 * og:E * r + 512 * (og + 1)], pb[r][:, :]),
                    r=[f"ps{r}"], w=[f"B{r}"], tag="y_ev")
        if dbg_stop == "p7a" and blk == nblk - 1:
            break
        for r in range(NTT):
            Ar = A[:, E * r:E * (r + 1)]
            Br = B[:, E * r:E * (r + 1)]
            Wr = wbF[:, 0:E]
            OP("scalar", lambda e, Br=Br: e.activation(
                xbf[:], Br[:, :], AF.Square, bias=cc(C_ZERO), scale=cc(C_ONE),
                accum_out=stats[:, 8:9]),
                r=[f"B{r}", "cst"], w=["st8", "xbf"], tag="rsq")
            OP("vector", lambda e: e.tensor_scalar(
                stats[:, 9:10], stats[:, 8:9], 1.0 / E, RMS_EPS, ALU.mult,
                ALU.add), r=["st8"], w=["st9"], small=True, tag="rms")
            OP("scalar", lambda e: e.activation(
                stats[:, 10:11], stats[:, 9:10], AF.Sqrt, bias=cc(C_ZERO),
                scale=cc(C_ONE)), r=["st9", "cst"], w=["st10"], small=True, tag="rsqt")
            OP("vector", lambda e: e.reciprocal(stats[:, 11:12],
                                                stats[:, 10:11]),
               r=["st10"], w=["st11"], small=True, tag="rinv")
            OP("vector", lambda e, Br=Br, Ar=Ar, Wr=Wr: e.scalar_tensor_tensor(
                Wr[:, :], Br[:, :], stats[:, 11:12], Ar[:, :], ALU.mult,
                ALU.add),
                r=[f"B{r}", "st11", f"A{r}"], w=WBALL, tag="resid2")
            OP("vector", lambda e, Wr=Wr: e.reduce_sum(stats[:, 0:1], Wr[:, :],
                                                       axis=AXX),
               r=WBALL, w=["st0"], small=True, tag="l2sum")
            OP("vector", lambda e: e.tensor_scalar(stats[:, 1:2], stats[:, 0:1],
                                                   1.0 / E, None, ALU.mult),
               r=["st0"], w=["st1"], small=True, tag="l2mean")
            OP("vector", lambda e, Ar=Ar, Wr=Wr: e.tensor_scalar(
                Ar[:, :], Wr[:, :], stats[:, 1:2], None, ALU.subtract),
                r=WBALL + ["st1"], w=[f"A{r}"], tag="l2ctr")
            OP("scalar", lambda e, Ar=Ar: e.activation(
                xbf[:], Ar[:, :], AF.Square, bias=cc(C_ZERO), scale=cc(C_ONE),
                accum_out=stats[:, 2:3]),
                r=[f"A{r}", "cst"], w=["st2", "xbf"], tag="l2sq")
            OP("vector", lambda e: e.tensor_scalar(
                stats[:, 3:4], stats[:, 2:3], 1.0 / E, LN_EPS, ALU.mult,
                ALU.add), r=["st2"], w=["st3"], small=True, tag="l2var")
            OP("scalar", lambda e: e.activation(
                stats[:, 4:5], stats[:, 3:4], AF.Sqrt, bias=cc(C_ZERO),
                scale=cc(C_ONE)), r=["st3", "cst"], w=["st4"], small=True, tag="l2std")
            OP("vector", lambda e: e.reciprocal(stats[:, 5:6], stats[:, 4:5]),
               r=["st4"], w=["st5"], small=True, tag="l2rstd")
            OP("vector", lambda e, Ar=Ar, Br=Br: e.tensor_scalar(
                Br[:, :], Ar[:, :], stats[:, 5:6], None, ALU.mult),
                r=[f"A{r}", "st5"], w=[f"B{r}"], tag="l2app")
            OP("gpsimd", lambda e, r=r, Br=Br, tb0=tb0: e.dma_start(
                out=y_d[tb0 + 128 * r:tb0 + 128 * r + 128, :], in_=Br[:, :]),
                r=[f"B{r}"], dma=True, tag="ystore")

    sch.barrier()
    tapmap = dict(A=A, B=B, xfm=xfm, qf=qf, es=es, af=af, kf0=kf0, kf1=kf1,
                  vt0=vt0, vt1=vt1, h=ar1, wb=wb, xbf=xbf, stats=stats, bc=bc,
                  sdn=sdnT_d, sgu=sguT_d, sq=sqT_d)
    for (name, ap_out) in tap_outs:
        key = name.split("__")[1]
        sap = tapmap[key]
        n_r = ap_out.shape[0]
        OP("gpsimd", lambda e, ap_out=ap_out, sap=sap, n_r=n_r: e.dma_start(
            out=ap_out, in_=sap[0:n_r, :]), r=[], dma=True, tag="tap")

    sch.barrier()
    print(f"[build] ops={len(sch.ops)} instrs~={sch.n_instr} "
          f"sbuf_free={nc.sbuf_bytes_remaining} psum_free={nc.psum_banks_remaining}")
    blk_cm = nc.Block()
    block = blk_cm.__enter__()
    sch.emit(block)
    blk_cm.__exit__(None, None, None)
    st.close()
    return nc


def _consts_np():
    ident = np.eye(128, dtype=ml_dtypes.bfloat16)
    onesc = np.ones((128, 1), dtype=ml_dtypes.bfloat16)
    onesr = np.ones((1, 128), dtype=ml_dtypes.bfloat16)
    r = np.arange(128)[:, None]
    u = np.arange(1024)[None, :]
    MP = (r >= (u - 512)).astype(ml_dtypes.bfloat16)
    MO = (r <= (u - 512)).astype(ml_dtypes.bfloat16)
    mask = np.concatenate([MP, MO], axis=1)
    cst = np.tile(np.array(CONSTS, dtype=np.float32), (128, 1))
    return ident, onesc, onesr, mask, cst


def _pi(n):
    out = np.zeros(n, dtype=np.int64)
    b0 = 0
    while b0 < n:
        w = min(1024, n - b0)
        e = np.arange(w)
        j, kn = e // 8, e % 8
        t, hbit = kn // 2, kn % 2
        ep = (w // 2) * hbit + 4 * j + t
        out[b0 + ep] = b0 + e
        b0 += w
    return out


def make_in_map(xb, wqkv, wout, gup, gus, dnp, dns):
    ident, onesc, onesr, mask, cst = _consts_np()
    pi_e = _pi(E)
    pi_i = _pi(NI)
    rows = np.concatenate([pi_i, NI + pi_i])
    return dict(x=np.ascontiguousarray(xb),
                wqkv=np.ascontiguousarray(np.asarray(wqkv)[:, pi_e]),
                wout=np.asarray(wout),
                gup=np.ascontiguousarray(np.asarray(gup)[rows]),
                gus=np.ascontiguousarray(np.asarray(gus)[:, rows]),
                dnp=np.asarray(dnp),
                dns=np.pad(np.asarray(dns), ((0, 4), (0, 0))),
                identin=ident, onescin=onesc,
                onesrin=onesr, maskin=mask, cstin=cst)


def kernel(L0_attn_x, L0_attn_w_qkv, L0_attn_w_out, L0_ffn_gate_up_packed,
           L0_ffn_gate_up_scales, L0_ffn_down_packed, L0_ffn_down_scales):
    nc = build_program(nblk=8)
    in_maps = [make_in_map(L0_attn_x[b], L0_attn_w_qkv, L0_attn_w_out,
                           L0_ffn_gate_up_packed, L0_ffn_gate_up_scales,
                           L0_ffn_down_packed, L0_ffn_down_scales)
               for b in range(8)]
    res = run_bass_kernel_spmd(nc, in_maps, core_ids=list(range(8)))
    return np.stack([np.asarray(r["y"], dtype=np.float32)
                     for r in res.results])


# version 05
# speedup vs baseline: 1.0838x; 1.0426x over previous
import numpy as np
import ml_dtypes
import contextlib
import concourse.bass as bass
import concourse.mybir as mybir
from concourse.bass_utils import run_bass_kernel_spmd

F32 = mybir.dt.float32
BF16 = mybir.dt.bfloat16
F16 = mybir.dt.float16
I32 = mybir.dt.int32
U8 = mybir.dt.uint8
AF = mybir.ActivationFunctionType
ALU = mybir.AluOpType
AXX = mybir.AxisListType.X

S = 4096
E = 2048
NH = 16
NI = 5632
TB = 512
NTT = 4
NEC = 16
NIC = 44
INV_SQRT_HD = 1.0 / float(np.sqrt(128.0))
LN_EPS = 1e-5
RMS_EPS = 1e-6

C_ONE, C_EXPSC, C_M8, C_M10, C_SGS, C_SGB, C_ZERO = range(7)
CONSTS = [1.0, INV_SQRT_HD, -8.0, -10.0, -1.0 / 16.0, 7.5 / 16.0, 0.0,
          0, 0, 0, 0, 0, 0, 0, 0, 0]

ENGS = ("sync", "gpsimd", "vector", "scalar", "tensor")


class Buf:
    __slots__ = ("name", "w", "readers", "w_small")

    def __init__(self, name):
        self.name = name
        self.w = None
        self.w_small = False
        self.readers = []


class Sem:
    __slots__ = ("name", "h", "n", "lazy")

    def __init__(self, name, h):
        self.name = name
        self.h = h
        self.n = 0
        self.lazy = False


class Sched:
    def __init__(self, nc=None, st=None):
        self.ops = []
        self.sems = {}
        self.water = {e: {} for e in ENGS}
        self.bufs = {}
        self.n_instr = 0
        self.nc = nc
        self.st = st

    def gb(self, name):
        if name not in self.bufs:
            self.bufs[name] = Buf(name)
        return self.bufs[name]

    def sem_for(self, eng, dma):
        if dma:
            key = "d_" + dma
            if key not in self.sems:
                h = self.st.enter_context(self.nc.semaphore(key))
                self.sems[key] = Sem(key, h)
            return self.sems[key]
        key = {"vector": "eV", "scalar": "eA", "tensor": "eT",
               "gpsimd": "eP"}[eng]
        return self.sems[key]

    def op(self, eng, emit, r=(), w=(), dma=False, lazy=False, small=False,
           tag=""):
        sem = self.sem_for(eng, dma)
        amt = 16 if dma else 1
        reads = [self.gb(x) for x in r]
        writes = [self.gb(x) for x in w]
        deps = []
        for b in reads:
            if b.w is not None:
                deps.append((b.w[0], b.w[1], b.w_small))
        for b in writes:
            if b.w is not None:
                deps.append((b.w[0], b.w[1], False))
            deps.extend((s_, v_, False) for (s_, v_) in b.readers)
        deps = [(s, s.n if s.name.startswith("d_") else v, f)
                for (s, v, f) in deps]
        need = {}
        own = None if dma else sem
        for (s, v, force) in deps:
            if s is own and not force:
                continue
            if need.get(s, -1) < v:
                need[s] = v
        wm = self.water[eng]
        waits = []
        for s, v in sorted(need.items(), key=lambda kv: kv[0].name):
            if wm.get(s, -1) >= v:
                continue
            wm[s] = v
            waits.append((s, v))
        if lazy:
            assert not dma
            val = sem.n + 1
            sem.lazy = True
        else:
            sem.n += amt
            val = sem.n
            sem.lazy = False
        for b in writes:
            b.w = (sem, val)
            b.w_small = small
            b.readers = []
        for b in reads:
            b.readers.append((sem, val))
        self.ops.append((eng, waits, emit, None if lazy else sem, amt, tag))
        self.n_instr += 1 + len(waits)

    def claim(self, eng, emitnop, names, tag="claim"):
        self.op(eng, emitnop, r=(), w=names, tag=tag)

    def barrier(self):
        for s in self.sems.values():
            assert not s.lazy, f"lazy pending on {s.name} at barrier"
        cur = [(s, s.n) for s in self.sems.values() if s.n > 0]
        for e in ENGS:
            wm = self.water[e]
            todo = [(s, v) for (s, v) in cur if wm.get(s, -1) < v]
            for (s, v) in todo:
                wm[s] = v
            if todo:
                self.ops.append((e, todo, None, None, 0, "barrier"))
                self.n_instr += len(todo)

    def emit(self, block):
        def mk(name):
            def body(eng):
                for (e, waits, emit, sem, amt, tag) in self.ops:
                    if e != name:
                        continue
                    for (s, v) in waits:
                        eng.wait_ge(s.h, v)
                    if emit is None:
                        continue
                    ins = emit(eng)
                    if sem is not None:
                        ins.then_inc(sem.h, amt)
            return body
        block.sync(mk("sync"))
        block.gpsimd(mk("gpsimd"))
        block.vector(mk("vector"))
        block.scalar(mk("scalar"))
        block.tensor(mk("tensor"))


def build_program(nblk=8, taps=None, dbg_stop=None, rd=True):
    nc = bass.Bass("TRN2", target_bir_lowering=False,
                   detect_race_conditions=rd)
    st = contextlib.ExitStack()

    x_d = nc.dram_tensor("x", [S, E], F32, kind="ExternalInput").ap()
    wqkv_d = nc.dram_tensor("wqkv", [3 * E, E], F32, kind="ExternalInput").ap()
    wout_d = nc.dram_tensor("wout", [E, E], F32, kind="ExternalInput").ap()
    gup_d = nc.dram_tensor("gup", [2 * NI, E // 8], I32, kind="ExternalInput").ap()
    gus_d = nc.dram_tensor("gus", [16, 2 * NI], F16, kind="ExternalInput").ap()
    dnp_d = nc.dram_tensor("dnp", [E, NI // 8], I32, kind="ExternalInput").ap()
    dns_d = nc.dram_tensor("dns", [48, E], F16, kind="ExternalInput").ap()
    ident_d = nc.dram_tensor("identin", [128, 128], BF16, kind="ExternalInput").ap()
    onesc_d = nc.dram_tensor("onescin", [128, 1], BF16, kind="ExternalInput").ap()
    onesr_d = nc.dram_tensor("onesrin", [1, 128], BF16, kind="ExternalInput").ap()
    mask_d = nc.dram_tensor("maskin", [128, 2048], BF16, kind="ExternalInput").ap()
    cst_d = nc.dram_tensor("cstin", [128, 16], F32, kind="ExternalInput").ap()
    y_d = nc.dram_tensor("y", [S, E], F32, kind="ExternalOutput").ap()
    sqT_d = nc.dram_tensor("scr_qkvT", [E, 3 * E], BF16, kind="Internal").ap()
    soT_d = nc.dram_tensor("scr_outT", [E, E], BF16, kind="Internal").ap()
    sguT_d = nc.dram_tensor("scr_guT", [E, 2 * NI], BF16, kind="Internal").ap()
    sdnT_d = nc.dram_tensor("scr_dnT", [NI, E], BF16, kind="Internal").ap()

    tap_outs = []
    if taps:
        for (name, shape, dtype) in taps:
            tap_outs.append((name, nc.dram_tensor(name, shape, dtype,
                                                  kind="ExternalOutput").ap()))

    sb = lambda name, shape, dt: st.enter_context(nc.sbuf_tensor(name, shape, dt))
    A = sb("A", [128, 4 * E], F32)
    B = sb("B", [128, 4 * E], F32)
    xbf = sb("xbf", [128, E], BF16)
    xfm = sb("xfm", [128, NEC * TB], BF16)
    kf0 = sb("kf0", [128, NH * TB], BF16)
    kf1 = sb("kf1", [128, NH * TB], BF16)
    vt0 = sb("vt0", [128, NTT * E], BF16)
    vt1 = sb("vt1", [128, NTT * E], BF16)
    ar1 = sb("ar1", [128, NIC * TB], BF16)
    wb = sb("wb", [128, 8 * TB], BF16)
    mkp = sb("mkp", [128, 1024], BF16)
    mko = sb("mko", [128, 1024], BF16)
    gs = sb("gs", [128, TB], BF16)
    ident = sb("ident", [128, 128], BF16)
    onescol = sb("onescol", [128, 1], BF16)
    onesrow = sb("onesrow", [1, 128], BF16)
    cst = sb("cst", [128, 16], F32)
    stats = sb("stats", [128, 16], F32)

    qf = ar1[:, 0:NH * TB]
    es = ar1[:, NH * TB:NH * TB + 4 * TB]
    af = ar1[:, 20 * TB:36 * TB]
    recb = ar1[:, 36 * TB:38 * TB].bitcast(F32)
    rh = ar1[:, 38 * TB:39 * TB][0:1, 0:512]
    rl = ar1[:, 39 * TB:40 * TB][0:1, 0:512]
    srow = ar1[:, 40 * TB:42 * TB].bitcast(F32)[0:1, 0:512]
    h_fm = ar1
    wbF = wb.bitcast(F32)

    AR1_ATTN = ["qf", "esA", "es0", "es1", "es2", "es3", "af", "rec",
                "srow", "rhi", "rlo"]

    pq = st.enter_context(nc.psum_tensor("pq", [128, 2048], F32))
    pb = [pq[:, 512 * i:512 * (i + 1)] for i in range(4)] + \
         [st.enter_context(nc.psum_tensor(f"pb{i}", [128, 512], F32))
          for i in range(4, 8)]
    pbh = {6: pb[6].bitcast(BF16), 7: pb[7].bitcast(BF16)}

    sch = Sched(nc, st)
    for sname in ("eV", "eA", "eT", "eP"):
        h = st.enter_context(nc.semaphore(sname))
        sch.sems[sname] = Sem(sname, h)

    OP = sch.op

    def cc(i):
        return cst[:, i:i + 1]

    nop = lambda e: e.engine_nop()

    OP("sync", lambda e: e.dma_start(out=ident[:], in_=ident_d), w=["ident"], dma="c")
    OP("sync", lambda e: e.dma_start(out=onescol[:], in_=onesc_d), w=["ones"], dma="c")
    OP("sync", lambda e: e.dma_start(out=onesrow[:], in_=onesr_d), w=["ones"], dma="c")
    OP("sync", lambda e: e.dma_start(out=mkp[:], in_=mask_d[:, 0:1024]), w=["mkp"], dma="c")
    OP("sync", lambda e: e.dma_start(out=mko[:], in_=mask_d[:, 1024:2048]), w=["mko"], dma="c")
    OP("sync", lambda e: e.dma_start(out=cst[:], in_=cst_d), w=["cst"], dma="c")

    stg = xfm

    def t_stage(dense, dense_tag, ec0, n_ec, osub, tag):
        for ecl in range(n_ec):
            ec = ec0 + ecl
            k = 6 + (ec % 2)
            OP("tensor", lambda e, ecl=ecl, k=k, dense=dense:
               e.transpose(pbh[k][:, 0:128],
                           dense[:, 128 * ecl:128 * (ecl + 1)], ident[:]),
               r=[dense_tag, "ident"], w=[f"ps{k}"], tag=tag + "T")
            OP("scalar", lambda e, ec=ec, osub=osub, k=k:
               e.copy(stg[:, 512 * ec + 128 * osub:512 * ec + 128 * osub + 128],
                      pbh[k][:, 0:128]),
               r=[f"ps{k}"], w=[f"stg{ec}"], tag=tag + "Tev")

    def t_flush(dstT, ocol0, tag):
        for ec in range(16):
            OP("gpsimd", lambda e, ec=ec, dstT=dstT, ocol0=ocol0: e.dma_start(
                out=dstT[128 * ec:128 * ec + 128, ocol0:ocol0 + 512],
                in_=stg[:, 512 * ec:512 * ec + 512]),
                r=[f"stg{ec}"], dma="fl0", tag=tag + "st")

    w32 = (kf0.bitcast(F32)[:, 0:E], kf1.bitcast(F32)[:, 0:E])
    dstg = (vt0[:, 0:E], vt1[:, 0:E])

    def cast_weights(src_d, dstT, nrows):
        for c in range(nrows // 128):
            sl = c % 2
            wsrc, wdst = w32[sl], dstg[sl]
            ldb, stb = f"wld{sl}", f"wst{sl}"
            OP("sync", lambda e, wsrc=wsrc, c=c, src_d=src_d: e.dma_start(
                out=wsrc[:, :], in_=src_d[128 * c:128 * c + 128, :]),
                w=[ldb], dma=f"wl{c % 2}", tag="wc_ld")
            OP("vector", lambda e, wsrc=wsrc, wdst=wdst:
               e.tensor_copy(wdst[:, :], wsrc[:, :]),
               r=[ldb], w=[stb], tag="wc_cp")
            t_stage(wdst, stb, 0, 16, c % 4, "wc")
            if c % 4 == 3:
                t_flush(dstT, 512 * (c // 4), "wc")

    cast_weights(wqkv_d, sqT_d, 3 * E)
    cast_weights(wout_d, soT_d, E)
    sch.barrier()

    spk = vt0.bitcast(I32)
    ssc = vt1.bitcast(F32)
    t_sgn = ar1[:, 2048:4096]
    t_m4 = ar1[:, 4096:6144]
    t_t4 = ar1[:, 6144:8192]
    t_t6 = ar1[:, 8192:10240]
    t_vv = ar1[:, 10240:12288]
    t_dense = ar1[:, 12288:14336]
    t_hq = ar1[:, 14336:16384].bitcast(I32)
    t_ub = ar1[:, 16384:16896]
    t_nib = ar1[:, 18432:19456]

    def dequant_tile(words_ap, nwords, scales_ap, store_fn, sctag):
        nb = 4 * nwords
        ne = 8 * nwords
        nib = t_nib[:, 0:ne]
        lb, hb = nib[:, 0:nb], nib[:, nb:ne]
        hq = t_hq[:, 0:nb]
        ubi = t_hq[:, 512:512 + nb]
        sgn, m4, t4, t6, vv = (t_sgn[:, 0:ne], t_m4[:, 0:ne], t_t4[:, 0:ne],
                               t_t6[:, 0:ne], t_vv[:, 0:ne])
        dense = t_dense[:, 0:ne]
        ub = t_ub[:, 0:nb]
        OP("vector", lambda e: e.tensor_copy(ubi[:, :], words_ap.bitcast(U8)),
           r=["dq_w"], w=["dq_ub"], tag="dq_ub")
        OP("vector", lambda e: e.tensor_scalar(hq[:, :], ubi[:, :], 4, None,
                                               ALU.logical_shift_right),
           r=["dq_ub"], w=["dq_hq"], tag="dq_hq")
        OP("vector", lambda e: e.tensor_copy(hb[:, :], hq[:, :]),
           r=["dq_hq"], w=["dq_nib"], tag="dq_hb")
        OP("vector", lambda e: e.tensor_copy(ub[:, :], ubi[:, :]),
           r=["dq_ub"], w=["dq_ubf"], tag="dq_ubf")
        OP("vector", lambda e: e.scalar_tensor_tensor(
            lb[:, :], hb[:, :], -16.0, ub[:, :], ALU.mult, ALU.add),
            r=["dq_nib", "dq_ubf"], w=["dq_nib"], tag="dq_lb")
        OP("scalar", lambda e: e.activation(sgn[:, :], nib[:, :], AF.Sign,
                                            bias=cc(C_SGB), scale=cc(C_SGS)),
           r=["dq_nib", "cst"], w=["dq_sgn"], tag="dq_sgn")
        OP("vector", lambda e: e.scalar_tensor_tensor(
            m4[:, :], sgn[:, :], 4.0, nib[:, :], ALU.mult, ALU.add),
            r=["dq_sgn", "dq_nib"], w=["dq_m4"], tag="dq_m4")
        OP("scalar", lambda e: e.activation(t4[:, :], m4[:, :], AF.Relu,
                                            bias=cc(C_M8), scale=cc(C_ONE)),
           r=["dq_m4", "cst"], w=["dq_t4"], tag="dq_t4")
        OP("scalar", lambda e: e.activation(t6[:, :], m4[:, :], AF.Relu,
                                            bias=cc(C_M10), scale=cc(C_ONE)),
           r=["dq_m4", "cst"], w=["dq_t6"], tag="dq_t6")
        OP("vector", lambda e: e.scalar_tensor_tensor(
            vv[:, :], m4[:, :], 4.0, t4[:, :], ALU.subtract, ALU.add),
            r=["dq_m4", "dq_t4"], w=["dq_vv"], tag="dq_w1")
        OP("vector", lambda e: e.scalar_tensor_tensor(
            t4[:, :], vv[:, :], 0.5, t6[:, :], ALU.mult, ALU.add),
            r=["dq_vv", "dq_t6"], w=["dq_t4"], tag="dq_v")
        OP("vector", lambda e: e.tensor_tensor(vv[:, :], t4[:, :], sgn[:, :],
                                               ALU.mult),
           r=["dq_t4", "dq_sgn"], w=["dq_vv"], tag="dq_val")
        ng = ne // 128
        v3 = vv.rearrange("p (hh c) -> p hh c", hh=2)
        d3 = dense.rearrange("p (hh c) -> p hh c", hh=2)
        for g in range(ng):
            OP("vector", lambda e, g=g: e.tensor_scalar(
                d3[:, :, 64 * g:64 * (g + 1)], v3[:, :, 64 * g:64 * (g + 1)],
                scales_ap[:, g:g + 1], None, ALU.mult),
                r=["dq_vv", sctag], w=["dq_dense"], lazy=(g < ng - 1), tag="dq_sc")
        store_fn(dense)

    sc_gu = ssc[:, 0:16]
    sc_guh = ssc[:, 16:24].bitcast(F16)[:, 0:16]
    for c in range(2 * NI // 128):
        wsl = spk[:, 0:256] if c % 2 == 0 else spk[:, 256:512]
        OP("sync", lambda e, c=c, wsl=wsl: e.dma_start(
            out=wsl[:, :], in_=gup_d[128 * c:128 * c + 128, :]),
            w=["dq_w"], dma=f"gw{c % 2}", tag="gu_ld")
        OP("sync", lambda e, c=c: e.dma_start(
            out=sc_guh[:, :], in_=gus_d[:, 128 * c:128 * c + 128],
            transpose=True), w=["dq_sch"], dma="gs0", tag="gu_scld")
        OP("vector", lambda e: e.tensor_copy(sc_gu[:, :], sc_guh[:, :]),
           r=["dq_sch"], w=["dq_sc_f"], small=True, tag="gu_scvt")
        for half in range(2):
            wv = wsl[:, 128 * half:128 * (half + 1)]

            def store_gu(dense, c=c, half=half):
                t_stage(dense, "dq_dense", 8 * half, 8, c % 4, "gu")
                if c % 4 == 3 and half == 1:
                    t_flush(sguT_d, 512 * (c // 4), "gu")

            dequant_tile(wv, 128, sc_gu[:, 8 * half:8 * half + 8],
                         store_gu, "dq_sc_f")

    sc_dn = ssc[:, 32:76]
    sc_dnh = ssc[:, 80:104].bitcast(F16)
    for c in range(E // 128):
        wsl = spk[:, 0:704] if c % 2 == 0 else spk[:, 1024:1728]
        OP("sync", lambda e, c=c, wsl=wsl: e.dma_start(
            out=wsl[:, :], in_=dnp_d[128 * c:128 * c + 128, :]),
            w=["dq_w"], dma=f"gw{c % 2}", tag="dn_ld")
        for tt in range(3):
            OP("sync", lambda e, c=c, tt=tt: e.dma_start(
                out=sc_dnh[:, 16 * tt:16 * tt + 16],
                in_=dns_d[16 * tt:16 * tt + 16, 128 * c:128 * c + 128],
                transpose=True), w=["dq_sch"], dma="gs0", tag="dn_scld")
        OP("vector", lambda e: e.tensor_copy(sc_dn[:, :], sc_dnh[:, 0:44]),
           r=["dq_sch"], w=["dq_sc_f"], small=True, tag="dn_scvt")
        for bq in range(6):
            nwords = 128 if bq < 5 else 64
            wv = wsl[:, 128 * bq:128 * bq + nwords]

            def store_dn(dense, c=c, bq=bq, nwords=nwords):
                for ecl in range(nwords // 16):
                    icg = 8 * bq + ecl
                    sd = kf0 if icg < 32 else kf1
                    scol = 256 * (icg % 32) + 128 * (c % 2)
                    k = 6 + (icg % 2)
                    OP("tensor", lambda e, ecl=ecl, k=k, dense=dense:
                       e.transpose(pbh[k][:, 0:128],
                                   dense[:, 128 * ecl:128 * (ecl + 1)],
                                   ident[:]),
                       r=["dq_dense", "ident"], w=[f"ps{k}"], tag="dnT")
                    OP("scalar", lambda e, sd=sd, scol=scol, k=k:
                       e.copy(sd[:, scol:scol + 128], pbh[k][:, 0:128]),
                       r=[f"ps{k}"], w=[f"std{icg}"], tag="dnTev")
                if c % 2 == 1:
                    for icg in range(8 * bq, 8 * bq + nwords // 16):
                        sd = kf0 if icg < 32 else kf1
                        OP("gpsimd", lambda e, icg=icg, sd=sd, c=c:
                           e.dma_start(
                               out=sdnT_d[128 * icg:128 * icg + 128,
                                          128 * (c - 1):128 * (c - 1) + 256],
                               in_=sd[:, 256 * (icg % 32):256 * (icg % 32) + 256]),
                           r=[f"std{icg}"], dma="fl1", tag="dn_st")

            dequant_tile(wv, nwords, sc_dn[:, 8 * bq:8 * bq + nwords // 16],
                         store_dn, "dq_sc_f")

    sch.barrier()

    wcount = [0]

    def stream_quad(srcT_d, kc0, ocol0, n=4):
        i = wcount[0] % 2
        wcount[0] += 1
        quad = wb[:, 2048 * i:2048 * i + 512 * n]
        src = srcT_d[128 * kc0:128 * kc0 + 128 * n,
                     ocol0:ocol0 + 512].rearrange("(a p) c -> p a c", a=n)
        dst = quad.rearrange("p (a c) -> p a c", a=n)
        eng = "sync" if i == 0 else "gpsimd"
        OP(eng, lambda e, dst=dst, src=src: e.dma_start(
            out=dst, in_=src),
            w=[f"wq{i}"], dma=f"wq{i}", tag="wstream")
        return quad, f"wq{i}"

    WBALL = ["wq0", "wq1"]

    def transpose_into_xfm(r, src_bf):
        x5 = src_bf.rearrange("p (b J t h) -> p b h J t", b=2, J=128, t=4, h=2)
        for ec in range(NEC):
            b_, hh, q = ec // 8, (ec % 8) // 4, ec % 4
            k = 6 + (ec % 2)
            OP("tensor", lambda e, b_=b_, hh=hh, q=q, k=k, x5=x5:
               e.transpose(pbh[k][:, 0:128],
                           x5[:, b_, hh, 32 * q:32 * (q + 1), :], ident[:]),
               r=["xbf", "ident"], w=[f"ps{k}"], tag="xT")
            OP("scalar", lambda e, ec=ec, r=r, k=k:
               e.copy(xfm[:, TB * ec + 128 * r:TB * ec + 128 * r + 128],
                      pbh[k][:, 0:128]),
               r=[f"ps{k}"], w=["xfm"], tag="xTev")

    kf = (kf0, kf1)
    vt = (vt0, vt1)

    for blk in range(nblk):
        par = blk % 2
        kfC, kfP = kf[par], kf[1 - par]
        vtC, vtP = vt[par], vt[1 - par]
        kfCt, kfPt = f"kf{par}", f"kf{1 - par}"
        vtCt, vtPt = f"vt{par}", f"vt{1 - par}"
        tb0 = blk * TB

        sch.claim("vector", nop, ["h"] + AR1_ATTN, tag="claimA")

        for r in range(NTT):
            Br = B[:, E * r:E * (r + 1)]
            OP("gpsimd", lambda e, r=r, Br=Br, tb0=tb0: e.dma_start(
                out=Br[:, :], in_=x_d[tb0 + 128 * r:tb0 + 128 * r + 128, :]),
                w=[f"B{r}"], dma=f"xl{r}", tag="xload")
            OP("vector", lambda e, Br=Br: e.tensor_copy(xbf[:], Br[:, :]),
               r=[f"B{r}"], w=["xbf"], tag="xcast")
            transpose_into_xfm(r, xbf)

        if dbg_stop == "p1" and blk == nblk - 1:
            break
        for fg in range(12):
            for kq in range(4):
                qslot, stag = stream_quad(sqT_d, 4 * kq, 512 * fg)
                for kcl in range(4):
                    kc = 4 * kq + kcl
                    slot = qslot[:, 512 * kcl:512 * kcl + 512]
                    for m in range(4):
                        first, last = kc == 0, kc == NEC - 1
                        if fg < 8:
                            OP("tensor", lambda e, slot=slot, kc=kc, m=m,
                               first=first, last=last: e.matmul(
                                   pb[m][:, :], slot[:, 128 * m:128 * (m + 1)],
                                   xfm[:, TB * kc:TB * (kc + 1)],
                                   start=first, stop=last),
                               r=[stag, "xfm"], w=[f"ps{m}"], lazy=(m < 3),
                               tag="qk_mm")
                        else:
                            OP("tensor", lambda e, slot=slot, kc=kc, m=m,
                               first=first, last=last: e.matmul(
                                   pb[m][:, :],
                                   xfm[:, TB * kc + 128 * m:TB * kc + 128 * (m + 1)],
                                   slot[:, :], start=first, stop=last),
                               r=[stag, "xfm"], w=[f"ps{m}"], lazy=(m < 3),
                               tag="v_mm")
            for m in range(4):
                if fg < 4:
                    hch = 4 * fg + m
                    OP("scalar", lambda e, hch=hch, m=m:
                       e.copy(qf[:, TB * hch:TB * (hch + 1)], pb[m][:, :]),
                       r=[f"ps{m}"], w=["qf"], tag="q_ev")
                elif fg < 8:
                    hch = 4 * (fg - 4) + m
                    OP("scalar", lambda e, hch=hch, m=m, kfC=kfC:
                       e.copy(kfC[:, TB * hch:TB * (hch + 1)], pb[m][:, :]),
                       r=[f"ps{m}"], w=[kfCt], tag="k_ev")
                else:
                    vg = fg - 8
                    OP("scalar", lambda e, vg=vg, m=m, vtC=vtC:
                       e.copy(vtC[:, E * m + 512 * vg:E * m + 512 * (vg + 1)],
                              pb[m][:, :]),
                       r=[f"ps{m}"], w=[vtCt], tag="v_ev")

        if dbg_stop == "p2" and blk == nblk - 1:
            break
        c_lo = 4 if blk == 0 else 0
        for hh in range(NH):
            pvb = 4 + (hh % 2)
            for half in range(c_lo // 4, 2):
                for cl in range(4):
                    c = 4 * half + cl
                    if c < 4:
                        ksrc, ktag = kfP, kfPt
                        kcol = TB * hh + 128 * c
                    else:
                        ksrc, ktag = kfC, kfCt
                        kcol = TB * hh + 128 * (c - 4)
                    OP("tensor", lambda e, ksrc=ksrc, kcol=kcol, cl=cl, hh=hh:
                       e.matmul(pb[cl][:, :], ksrc[:, kcol:kcol + 128],
                                qf[:, TB * hh:TB * (hh + 1)], start=True,
                                stop=True),
                       r=[ktag, "qf"], w=[f"ps{cl}"], lazy=(cl < 3),
                       tag="score")
                OP("scalar", lambda e: e.activation(
                    es[:, :], pq[:, :], AF.Exp, bias=cc(C_ZERO),
                    scale=cc(C_EXPSC)),
                    r=["ps0", "ps1", "ps2", "ps3", "cst"],
                    w=["esA", "es0", "es1", "es2", "es3"], tag="exp")
                for cl in range(4):
                    c = 4 * half + cl
                    if c < 4:
                        msrc, mt = mkp, "mkp"
                        mcol = 512 - 128 * c
                        vsrc, vtag = vtP, vtPt
                        vcol = E * c + 128 * hh
                    else:
                        msrc, mt = mko, "mko"
                        mcol = 512 - 128 * (c - 4)
                        vsrc, vtag = vtC, vtCt
                        vcol = E * (c - 4) + 128 * hh
                    OP("vector", lambda e, cl=cl, msrc=msrc, mcol=mcol:
                       e.tensor_tensor(es[:, 512 * cl:512 * cl + 512],
                                       es[:, 512 * cl:512 * cl + 512],
                                       msrc[:, mcol:mcol + 512], ALU.mult),
                       r=["esA", mt], w=[f"es{cl}"], tag="mask")
                    OP("tensor", lambda e, cl=cl, c=c, c_lo=c_lo: e.matmul(
                        pb[6][0:1, :], onescol[:, 0:1],
                        es[:, 512 * cl:512 * cl + 512],
                        start=(c == c_lo), stop=(c == 7)),
                        r=[f"es{cl}", "ones"], w=["ps6"], lazy=(c < 7),
                        tag="summ")
                    OP("tensor", lambda e, vsrc=vsrc, vcol=vcol, cl=cl, c=c,
                       c_lo=c_lo, pvb=pvb:
                       e.matmul(pb[pvb][:, :], vsrc[:, vcol:vcol + 128],
                                es[:, 512 * cl:512 * cl + 512],
                                start=(c == c_lo), stop=(c == 7)),
                       r=[vtag, f"es{cl}"], w=[f"ps{pvb}"], lazy=(c < 7),
                       tag="pv")
            OP("scalar", lambda e: e.copy(srow[:, :], pb[6][0:1, :]),
               r=["ps6"], w=["srow"], small=True, tag="sum_ev")
            OP("vector", lambda e: e.tensor_copy(rh[:, :], srow[:, :]),
               r=["srow"], w=["rhi"], small=True, tag="rhi")
            OP("vector", lambda e: e.scalar_tensor_tensor(
                rl[:, :], rh[:, :], -1.0, srow[:, :], ALU.mult, ALU.add),
                r=["rhi", "srow"], w=["rlo"], small=True, tag="rlo")
            OP("tensor", lambda e: e.matmul(pb[7][:, :], onesrow[:, :],
                                            rh[:, :], start=True, stop=False),
               r=["rhi", "ones"], w=["ps7"], lazy=True, tag="bch")
            OP("tensor", lambda e: e.matmul(pb[7][:, :], onesrow[:, :],
                                            rl[:, :], start=False, stop=True),
               r=["rlo", "ones"], w=["ps7"], tag="bcl")
            OP("vector", lambda e: e.reciprocal(recb[:, :], pb[7][:, :]),
               r=["ps7"], w=["rec"], tag="recip")
            OP("vector", lambda e, hh=hh, pvb=pvb: e.scalar_tensor_tensor(
                af[:, TB * hh:TB * (hh + 1)], pb[pvb][:, :], 1.0, recb[:, :],
                ALU.bypass, ALU.mult),
                r=[f"ps{pvb}", "rec"], w=["af"], tag="attn_ev")

        if dbg_stop == "p3" and blk == nblk - 1:
            break
        for og in range(4):
            for kq in range(4):
                qslot, stag = stream_quad(soT_d, 4 * kq, 512 * og)
                for kcl in range(4):
                    kc = 4 * kq + kcl
                    slot = qslot[:, 512 * kcl:512 * kcl + 512]
                    for m in range(4):
                        OP("tensor", lambda e, slot=slot, kc=kc, m=m:
                           e.matmul(pb[m][:, :],
                                    af[:, TB * kc + 128 * m:TB * kc + 128 * (m + 1)],
                                    slot[:, :], start=(kc == 0),
                                    stop=(kc == NEC - 1)),
                           r=[stag, "af"], w=[f"ps{m}"], lazy=(m < 3),
                           tag="pj_mm")
            for r in range(4):
                OP("vector", lambda e, r=r, og=og: e.scalar_tensor_tensor(
                    A[:, E * r + 512 * og:E * r + 512 * (og + 1)], pb[r][:, :],
                    1.0, B[:, E * r + 512 * og:E * r + 512 * (og + 1)],
                    ALU.bypass, ALU.add),
                    r=[f"ps{r}", f"B{r}"], w=[f"A{r}"], tag="resid1")
        if dbg_stop == "p4a" and blk == nblk - 1:
            break
        for r in range(NTT):
            Ar = A[:, E * r:E * (r + 1)]
            Br = B[:, E * r:E * (r + 1)]
            OP("vector", lambda e, Ar=Ar: e.reduce_sum(stats[:, 0:1], Ar[:, :],
                                                       axis=AXX),
               r=[f"A{r}"], w=["st0"], small=True, tag="l1sum")
            OP("vector", lambda e: e.tensor_scalar(stats[:, 1:2], stats[:, 0:1],
                                                   1.0 / E, None, ALU.mult),
               r=["st0"], w=["st1"], small=True, tag="l1mean")
            OP("vector", lambda e, Ar=Ar, Br=Br: e.tensor_scalar(
                Br[:, :], Ar[:, :], stats[:, 1:2], None, ALU.subtract),
                r=[f"A{r}", "st1"], w=[f"B{r}"], tag="l1ctr")
            OP("scalar", lambda e, Br=Br: e.activation(
                xbf[:], Br[:, :], AF.Square, bias=cc(C_ZERO), scale=cc(C_ONE),
                accum_out=stats[:, 2:3]),
                r=[f"B{r}", "cst"], w=["st2", "xbf"], tag="l1sq")
            OP("vector", lambda e: e.tensor_scalar(
                stats[:, 3:4], stats[:, 2:3], 1.0 / E, LN_EPS, ALU.mult,
                ALU.add), r=["st2"], w=["st3"], small=True, tag="l1var")
            OP("scalar", lambda e: e.activation(
                stats[:, 4:5], stats[:, 3:4], AF.Sqrt, bias=cc(C_ZERO),
                scale=cc(C_ONE)), r=["st3", "cst"], w=["st4"], small=True, tag="l1std")
            OP("vector", lambda e: e.reciprocal(stats[:, 5:6], stats[:, 4:5]),
               r=["st4"], w=["st5"], small=True, tag="l1rstd")
            OP("vector", lambda e, Ar=Ar, Br=Br: e.tensor_scalar(
                Ar[:, :], Br[:, :], stats[:, 5:6], None, ALU.mult),
                r=[f"B{r}", "st5"], w=[f"A{r}"], tag="l1app")
            OP("vector", lambda e, Ar=Ar: e.tensor_copy(xbf[:], Ar[:, :]),
               r=[f"A{r}"], w=["xbf"], tag="h1cast")
            transpose_into_xfm(r, xbf)

        if dbg_stop == "p5" and blk == nblk - 1:
            break
        sch.claim("vector", nop, AR1_ATTN + ["h"], tag="claimH")

        for g2 in range(NI // 512):
            for side in range(2):
                bks = (0, 1, 2, 3) if side == 0 else (4, 5, 6, 7)
                for kq in range(4):
                    qslot, stag = stream_quad(sguT_d, 4 * kq,
                                              NI * side + 512 * g2)
                    for kcl in range(4):
                        kc = 4 * kq + kcl
                        slot = qslot[:, 512 * kcl:512 * kcl + 512]
                        for m in range(4):
                            OP("tensor", lambda e, slot=slot, kc=kc, m=m,
                               bks=bks: e.matmul(
                                   pb[bks[m]][:, :],
                                   slot[:, 128 * m:128 * (m + 1)],
                                   xfm[:, TB * kc:TB * (kc + 1)],
                                   start=(kc == 0), stop=(kc == NEC - 1)),
                               r=[stag, "xfm"], w=[f"ps{bks[m]}"],
                               lazy=(m < 3), tag="gu_mm")
            for m in range(4):
                ic = 4 * g2 + m
                OP("scalar", lambda e, m=m: e.activation(
                    gs[:], pb[m][:, :], AF.Silu, bias=cc(C_ZERO),
                    scale=cc(C_ONE)),
                    r=[f"ps{m}", "cst"], w=["gs"], tag="silu")
                OP("vector", lambda e, ic=ic, m=m: e.tensor_tensor(
                    h_fm[:, TB * ic:TB * (ic + 1)], gs[:], pb[4 + m][:, :],
                    ALU.mult), r=["gs", f"ps{4 + m}"], w=["h"], tag="silmul")

        if dbg_stop == "p6" and blk == nblk - 1:
            break
        for og in range(4):
            for icq in range(NIC // 4):
                qslot, stag = stream_quad(sdnT_d, 4 * icq, 512 * og)
                for icl in range(4):
                    ic = 4 * icq + icl
                    slot = qslot[:, 512 * icl:512 * icl + 512]
                    for r in range(4):
                        OP("tensor", lambda e, slot=slot, ic=ic, r=r:
                           e.matmul(pb[r][:, :],
                                    h_fm[:, TB * ic + 128 * r:TB * ic + 128 * (r + 1)],
                                    slot[:, :], start=(ic == 0),
                                    stop=(ic == NIC - 1)),
                           r=[stag, "h"], w=[f"ps{r}"], lazy=(r < 3),
                           tag="dn_mm")
            for r in range(4):
                OP("scalar", lambda e, r=r, og=og: e.copy(
                    B[:, E * r + 512 * og:E * r + 512 * (og + 1)], pb[r][:, :]),
                    r=[f"ps{r}"], w=[f"B{r}"], tag="y_ev")
        if dbg_stop == "p7a" and blk == nblk - 1:
            break
        for r in range(NTT):
            Ar = A[:, E * r:E * (r + 1)]
            Br = B[:, E * r:E * (r + 1)]
            Wr = wbF[:, 0:E]
            OP("scalar", lambda e, Br=Br: e.activation(
                xbf[:], Br[:, :], AF.Square, bias=cc(C_ZERO), scale=cc(C_ONE),
                accum_out=stats[:, 8:9]),
                r=[f"B{r}", "cst"], w=["st8", "xbf"], tag="rsq")
            OP("vector", lambda e: e.tensor_scalar(
                stats[:, 9:10], stats[:, 8:9], 1.0 / E, RMS_EPS, ALU.mult,
                ALU.add), r=["st8"], w=["st9"], small=True, tag="rms")
            OP("scalar", lambda e: e.activation(
                stats[:, 10:11], stats[:, 9:10], AF.Sqrt, bias=cc(C_ZERO),
                scale=cc(C_ONE)), r=["st9", "cst"], w=["st10"], small=True, tag="rsqt")
            OP("vector", lambda e: e.reciprocal(stats[:, 11:12],
                                                stats[:, 10:11]),
               r=["st10"], w=["st11"], small=True, tag="rinv")
            OP("vector", lambda e, Br=Br, Ar=Ar, Wr=Wr: e.scalar_tensor_tensor(
                Wr[:, :], Br[:, :], stats[:, 11:12], Ar[:, :], ALU.mult,
                ALU.add),
                r=[f"B{r}", "st11", f"A{r}"], w=WBALL, tag="resid2")
            OP("vector", lambda e, Wr=Wr: e.reduce_sum(stats[:, 0:1], Wr[:, :],
                                                       axis=AXX),
               r=WBALL, w=["st0"], small=True, tag="l2sum")
            OP("vector", lambda e: e.tensor_scalar(stats[:, 1:2], stats[:, 0:1],
                                                   1.0 / E, None, ALU.mult),
               r=["st0"], w=["st1"], small=True, tag="l2mean")
            OP("vector", lambda e, Ar=Ar, Wr=Wr: e.tensor_scalar(
                Ar[:, :], Wr[:, :], stats[:, 1:2], None, ALU.subtract),
                r=WBALL + ["st1"], w=[f"A{r}"], tag="l2ctr")
            OP("scalar", lambda e, Ar=Ar: e.activation(
                xbf[:], Ar[:, :], AF.Square, bias=cc(C_ZERO), scale=cc(C_ONE),
                accum_out=stats[:, 2:3]),
                r=[f"A{r}", "cst"], w=["st2", "xbf"], tag="l2sq")
            OP("vector", lambda e: e.tensor_scalar(
                stats[:, 3:4], stats[:, 2:3], 1.0 / E, LN_EPS, ALU.mult,
                ALU.add), r=["st2"], w=["st3"], small=True, tag="l2var")
            OP("scalar", lambda e: e.activation(
                stats[:, 4:5], stats[:, 3:4], AF.Sqrt, bias=cc(C_ZERO),
                scale=cc(C_ONE)), r=["st3", "cst"], w=["st4"], small=True, tag="l2std")
            OP("vector", lambda e: e.reciprocal(stats[:, 5:6], stats[:, 4:5]),
               r=["st4"], w=["st5"], small=True, tag="l2rstd")
            OP("vector", lambda e, Ar=Ar, Br=Br: e.tensor_scalar(
                Br[:, :], Ar[:, :], stats[:, 5:6], None, ALU.mult),
                r=[f"A{r}", "st5"], w=[f"B{r}"], tag="l2app")
            OP("gpsimd", lambda e, r=r, Br=Br, tb0=tb0: e.dma_start(
                out=y_d[tb0 + 128 * r:tb0 + 128 * r + 128, :], in_=Br[:, :]),
                r=[f"B{r}"], dma=f"ys{r}", tag="ystore")

    sch.barrier()
    tapmap = dict(A=A, B=B, xfm=xfm, qf=qf, es=es, af=af, kf0=kf0, kf1=kf1,
                  vt0=vt0, vt1=vt1, h=ar1, wb=wb, xbf=xbf, stats=stats,
                  sdn=sdnT_d, sgu=sguT_d, sq=sqT_d)
    for (name, ap_out) in tap_outs:
        key = name.split("__")[1]
        sap = tapmap[key]
        n_r = ap_out.shape[0]
        OP("gpsimd", lambda e, ap_out=ap_out, sap=sap, n_r=n_r: e.dma_start(
            out=ap_out, in_=sap[0:n_r, :]), r=[], dma="tap", tag="tap")

    sch.barrier()
    print(f"[build] ops={len(sch.ops)} instrs~={sch.n_instr} "
          f"sbuf_free={nc.sbuf_bytes_remaining} psum_free={nc.psum_banks_remaining}")
    blk_cm = nc.Block()
    block = blk_cm.__enter__()
    sch.emit(block)
    blk_cm.__exit__(None, None, None)
    st.close()
    return nc


def _consts_np():
    ident = np.eye(128, dtype=ml_dtypes.bfloat16)
    onesc = np.ones((128, 1), dtype=ml_dtypes.bfloat16)
    onesr = np.ones((1, 128), dtype=ml_dtypes.bfloat16)
    r = np.arange(128)[:, None]
    u = np.arange(1024)[None, :]
    MP = (r >= (u - 512)).astype(ml_dtypes.bfloat16)
    MO = (r <= (u - 512)).astype(ml_dtypes.bfloat16)
    mask = np.concatenate([MP, MO], axis=1)
    cst = np.tile(np.array(CONSTS, dtype=np.float32), (128, 1))
    return ident, onesc, onesr, mask, cst


def _pi(n):
    out = np.zeros(n, dtype=np.int64)
    b0 = 0
    while b0 < n:
        w = min(1024, n - b0)
        e = np.arange(w)
        j, kn = e // 8, e % 8
        t, hbit = kn // 2, kn % 2
        ep = (w // 2) * hbit + 4 * j + t
        out[b0 + ep] = b0 + e
        b0 += w
    return out


def make_in_map(xb, wqkv, wout, gup, gus, dnp, dns):
    ident, onesc, onesr, mask, cst = _consts_np()
    pi_e = _pi(E)
    pi_i = _pi(NI)
    rows = np.concatenate([pi_i, NI + pi_i])
    return dict(x=np.ascontiguousarray(xb),
                wqkv=np.ascontiguousarray(np.asarray(wqkv)[:, pi_e]),
                wout=np.asarray(wout),
                gup=np.ascontiguousarray(np.asarray(gup)[rows]),
                gus=np.ascontiguousarray(np.asarray(gus)[:, rows]),
                dnp=np.asarray(dnp),
                dns=np.pad(np.asarray(dns), ((0, 4), (0, 0))),
                identin=ident, onescin=onesc,
                onesrin=onesr, maskin=mask, cstin=cst)


def kernel(L0_attn_x, L0_attn_w_qkv, L0_attn_w_out, L0_ffn_gate_up_packed,
           L0_ffn_gate_up_scales, L0_ffn_down_packed, L0_ffn_down_scales):
    nc = build_program(nblk=8)
    in_maps = [make_in_map(L0_attn_x[b], L0_attn_w_qkv, L0_attn_w_out,
                           L0_ffn_gate_up_packed, L0_ffn_gate_up_scales,
                           L0_ffn_down_packed, L0_ffn_down_scales)
               for b in range(8)]
    res = run_bass_kernel_spmd(nc, in_maps, core_ids=list(range(8)))
    return np.stack([np.asarray(r["y"], dtype=np.float32)
                     for r in res.results])


# version 06
# speedup vs baseline: 1.1494x; 1.0606x over previous
import numpy as np
import ml_dtypes
import contextlib
import concourse.bass as bass
import concourse.mybir as mybir
from concourse.bass_utils import run_bass_kernel_spmd

F32 = mybir.dt.float32
BF16 = mybir.dt.bfloat16
F16 = mybir.dt.float16
I32 = mybir.dt.int32
U8 = mybir.dt.uint8
AF = mybir.ActivationFunctionType
ALU = mybir.AluOpType
AXX = mybir.AxisListType.X

S = 4096
E = 2048
NH = 16
NI = 5632
TB = 512
NTT = 4
NEC = 16
NIC = 44
INV_SQRT_HD = 1.0 / float(np.sqrt(128.0))
LN_EPS = 1e-5
RMS_EPS = 1e-6

C_ONE, C_EXPSC, C_M8, C_M10, C_SGS, C_SGB, C_ZERO = range(7)
CONSTS = [1.0, INV_SQRT_HD, -8.0, -10.0, -1.0 / 16.0, 7.5 / 16.0, 0.0,
          0, 0, 0, 0, 0, 0, 0, 0, 0]

ENGS = ("sync", "gpsimd", "vector", "scalar", "tensor")


class Buf:
    __slots__ = ("name", "w", "readers", "w_small")

    def __init__(self, name):
        self.name = name
        self.w = None
        self.w_small = False
        self.readers = []


class Sem:
    __slots__ = ("name", "h", "n", "lazy")

    def __init__(self, name, h):
        self.name = name
        self.h = h
        self.n = 0
        self.lazy = False


class Sched:
    def __init__(self, nc=None, st=None):
        self.ops = []
        self.sems = {}
        self.water = {e: {} for e in ENGS}
        self.bufs = {}
        self.n_instr = 0
        self.nc = nc
        self.st = st

    def gb(self, name):
        if name not in self.bufs:
            self.bufs[name] = Buf(name)
        return self.bufs[name]

    def sem_for(self, eng, dma):
        if dma:
            key = "d_" + dma
            if key not in self.sems:
                h = self.st.enter_context(self.nc.semaphore(key))
                self.sems[key] = Sem(key, h)
            return self.sems[key]
        key = {"vector": "eV", "scalar": "eA", "tensor": "eT",
               "gpsimd": "eP"}[eng]
        return self.sems[key]

    def op(self, eng, emit, r=(), w=(), dma=False, lazy=False, small=False,
           tag=""):
        sem = self.sem_for(eng, dma)
        amt = 16 if dma else 1
        reads = [self.gb(x) for x in r]
        writes = [self.gb(x) for x in w]
        deps = []
        for b in reads:
            if b.w is not None:
                deps.append((b.w[0], b.w[1], b.w_small))
        for b in writes:
            if b.w is not None:
                deps.append((b.w[0], b.w[1], False))
            deps.extend((s_, v_, False) for (s_, v_) in b.readers)
        deps = [(s, s.n if s.name.startswith("d_") else v, f)
                for (s, v, f) in deps]
        need = {}
        own = None if dma else sem
        for (s, v, force) in deps:
            if s is own and not force:
                continue
            if need.get(s, -1) < v:
                need[s] = v
        wm = self.water[eng]
        waits = []
        for s, v in sorted(need.items(), key=lambda kv: kv[0].name):
            if wm.get(s, -1) >= v:
                continue
            wm[s] = v
            waits.append((s, v))
        if lazy:
            assert not dma
            val = sem.n + 1
            sem.lazy = True
        else:
            sem.n += amt
            val = sem.n
            sem.lazy = False
        for b in writes:
            b.w = (sem, val)
            b.w_small = small
            b.readers = []
        for b in reads:
            b.readers.append((sem, val))
        self.ops.append((eng, waits, emit, None if lazy else sem, amt, tag))
        self.n_instr += 1 + len(waits)

    def claim(self, eng, emitnop, names, tag="claim"):
        self.op(eng, emitnop, r=(), w=names, tag=tag)

    def barrier(self):
        for s in self.sems.values():
            assert not s.lazy, f"lazy pending on {s.name} at barrier"
        cur = [(s, s.n) for s in self.sems.values() if s.n > 0]
        for e in ENGS:
            wm = self.water[e]
            todo = [(s, v) for (s, v) in cur if wm.get(s, -1) < v]
            for (s, v) in todo:
                wm[s] = v
            if todo:
                self.ops.append((e, todo, None, None, 0, "barrier"))
                self.n_instr += len(todo)

    def emit(self, block):
        def mk(name):
            def body(eng):
                for (e, waits, emit, sem, amt, tag) in self.ops:
                    if e != name:
                        continue
                    for (s, v) in waits:
                        eng.wait_ge(s.h, v)
                    if emit is None:
                        continue
                    ins = emit(eng)
                    if sem is not None:
                        ins.then_inc(sem.h, amt)
            return body
        block.sync(mk("sync"))
        block.gpsimd(mk("gpsimd"))
        block.vector(mk("vector"))
        block.scalar(mk("scalar"))
        block.tensor(mk("tensor"))


def build_program(nblk=8, taps=None, dbg_stop=None, rd=True):
    nc = bass.Bass("TRN2", target_bir_lowering=False,
                   detect_race_conditions=rd)
    st = contextlib.ExitStack()

    x_d = nc.dram_tensor("x", [S, E], F32, kind="ExternalInput").ap()
    wqkv_d = nc.dram_tensor("wqkv", [3 * E, E], F32, kind="ExternalInput").ap()
    wout_d = nc.dram_tensor("wout", [E, E], F32, kind="ExternalInput").ap()
    gup_d = nc.dram_tensor("gup", [2 * NI, E // 8], I32, kind="ExternalInput").ap()
    gus_d = nc.dram_tensor("gus", [16, 2 * NI], F16, kind="ExternalInput").ap()
    dnp_d = nc.dram_tensor("dnp", [E, NI // 8], I32, kind="ExternalInput").ap()
    dns_d = nc.dram_tensor("dns", [48, E], F16, kind="ExternalInput").ap()
    ident_d = nc.dram_tensor("identin", [128, 128], BF16, kind="ExternalInput").ap()
    onesc_d = nc.dram_tensor("onescin", [128, 1], BF16, kind="ExternalInput").ap()
    onesr_d = nc.dram_tensor("onesrin", [1, 128], BF16, kind="ExternalInput").ap()
    mask_d = nc.dram_tensor("maskin", [128, 2048], BF16, kind="ExternalInput").ap()
    cst_d = nc.dram_tensor("cstin", [128, 16], F32, kind="ExternalInput").ap()
    y_d = nc.dram_tensor("y", [S, E], F32, kind="ExternalOutput").ap()
    sqP_d = nc.dram_tensor("scr_qP", [4 * 12 * 128, 2048], BF16, kind="Internal").ap()
    soP_d = nc.dram_tensor("scr_oP", [4 * 4 * 128, 2048], BF16, kind="Internal").ap()
    sguP_d = nc.dram_tensor("scr_guP", [4 * 22 * 128, 2048], BF16, kind="Internal").ap()
    sdnP_d = nc.dram_tensor("scr_dnP", [11 * 4 * 128, 2048], BF16, kind="Internal").ap()

    tap_outs = []
    if taps:
        for (name, shape, dtype) in taps:
            tap_outs.append((name, nc.dram_tensor(name, shape, dtype,
                                                  kind="ExternalOutput").ap()))

    sb = lambda name, shape, dt: st.enter_context(nc.sbuf_tensor(name, shape, dt))
    A = sb("A", [128, 4 * E], F32)
    B = sb("B", [128, 4 * E], F32)
    xbf = sb("xbf", [128, E], BF16)
    xfm = sb("xfm", [128, NEC * TB], BF16)
    kf0 = sb("kf0", [128, NH * TB], BF16)
    kf1 = sb("kf1", [128, NH * TB], BF16)
    vt0 = sb("vt0", [128, NTT * E], BF16)
    vt1 = sb("vt1", [128, NTT * E], BF16)
    ar1 = sb("ar1", [128, NIC * TB], BF16)
    wb = sb("wb", [128, 8 * TB], BF16)
    mkp = sb("mkp", [128, 1024], BF16)
    mko = sb("mko", [128, 1024], BF16)
    gs = sb("gs", [128, TB], BF16)
    ident = sb("ident", [128, 128], BF16)
    onescol = sb("onescol", [128, 1], BF16)
    onesrow = sb("onesrow", [1, 128], BF16)
    cst = sb("cst", [128, 16], F32)
    stats = sb("stats", [128, 64], F32)

    qf = ar1[:, 0:NH * TB]
    es = ar1[:, NH * TB:NH * TB + 4 * TB]
    af = ar1[:, 20 * TB:36 * TB]
    recb = ar1[:, 36 * TB:38 * TB].bitcast(F32)
    rh = ar1[:, 38 * TB:39 * TB][0:1, 0:512]
    rl = ar1[:, 39 * TB:40 * TB][0:1, 0:512]
    srow = ar1[:, 40 * TB:42 * TB].bitcast(F32)[0:1, 0:512]
    h_fm = ar1
    wbF = ar1[:, 8192:16384].bitcast(F32)

    AR1_ATTN = ["qf", "esA", "es0", "es1", "es2", "es3", "af", "rec",
                "srow", "rhi", "rlo"]

    pq = st.enter_context(nc.psum_tensor("pq", [128, 2048], F32))
    pb = [pq[:, 512 * i:512 * (i + 1)] for i in range(4)] + \
         [st.enter_context(nc.psum_tensor(f"pb{i}", [128, 512], F32))
          for i in range(4, 8)]
    pbh = {6: pb[6].bitcast(BF16), 7: pb[7].bitcast(BF16)}

    sch = Sched(nc, st)
    for sname in ("eV", "eA", "eT", "eP"):
        h = st.enter_context(nc.semaphore(sname))
        sch.sems[sname] = Sem(sname, h)

    OP = sch.op

    def cc(i):
        return cst[:, i:i + 1]

    nop = lambda e: e.engine_nop()

    OP("sync", lambda e: e.dma_start(out=ident[:], in_=ident_d), w=["ident"], dma="c")
    OP("sync", lambda e: e.dma_start(out=onescol[:], in_=onesc_d), w=["ones"], dma="c")
    OP("sync", lambda e: e.dma_start(out=onesrow[:], in_=onesr_d), w=["ones"], dma="c")
    OP("sync", lambda e: e.dma_start(out=mkp[:], in_=mask_d[:, 0:1024]), w=["mkp"], dma="c")
    OP("sync", lambda e: e.dma_start(out=mko[:], in_=mask_d[:, 1024:2048]), w=["mko"], dma="c")
    OP("sync", lambda e: e.dma_start(out=cst[:], in_=cst_d), w=["cst"], dma="c")

    stg = xfm

    def t_stage(dense, dense_tag, ec0, n_ec, osub, tag):
        for ecl in range(n_ec):
            ec = ec0 + ecl
            k = 6 + (ec % 2)
            OP("tensor", lambda e, ecl=ecl, k=k, dense=dense:
               e.transpose(pbh[k][:, 0:128],
                           dense[:, 128 * ecl:128 * (ecl + 1)], ident[:]),
               r=[dense_tag, "ident"], w=[f"ps{k}"], tag=tag + "T")
            OP("scalar", lambda e, ec=ec, osub=osub, k=k:
               e.copy(stg[:, 512 * ec + 128 * osub:512 * ec + 128 * osub + 128],
                      pbh[k][:, 0:128]),
               r=[f"ps{k}"], w=[f"stg{ec}"], tag=tag + "Tev")

    def t_flush(dstP, nf, fg, tag):
        for ec in range(16):
            rb = ((ec // 4) * nf + fg) * 128
            OP("gpsimd", lambda e, rb=rb, ec=ec, dstP=dstP: e.dma_start(
                out=dstP[rb:rb + 128, 512 * (ec % 4):512 * (ec % 4) + 512],
                in_=stg[:, 512 * ec:512 * ec + 512]),
                r=[f"stg{ec}"], dma="fl0", tag=tag + "st")

    w32 = (kf0.bitcast(F32)[:, 0:E], kf1.bitcast(F32)[:, 0:E])
    dstg = (vt0[:, 0:E], vt1[:, 0:E])

    def cast_weights(src_d, dstP, nf, nrows):
        for c in range(nrows // 128):
            sl = c % 2
            wsrc, wdst = w32[sl], dstg[sl]
            ldb, stb = f"wld{sl}", f"wst{sl}"
            OP("sync", lambda e, wsrc=wsrc, c=c, src_d=src_d: e.dma_start(
                out=wsrc[:, :], in_=src_d[128 * c:128 * c + 128, :]),
                w=[ldb], dma=f"wl{c % 2}", tag="wc_ld")
            OP("vector", lambda e, wsrc=wsrc, wdst=wdst:
               e.tensor_copy(wdst[:, :], wsrc[:, :]),
               r=[ldb], w=[stb], tag="wc_cp")
            t_stage(wdst, stb, 0, 16, c % 4, "wc")
            if c % 4 == 3:
                t_flush(dstP, nf, c // 4, "wc")

    cast_weights(wqkv_d, sqP_d, 12, 3 * E)
    cast_weights(wout_d, soP_d, 4, E)
    sch.barrier()

    spk = vt0.bitcast(I32)
    ssc = vt1.bitcast(F32)
    t_sgn = ar1[:, 2048:4096]
    t_m4 = ar1[:, 4096:6144]
    t_t4 = ar1[:, 6144:8192]
    t_t6 = ar1[:, 8192:10240]
    t_vv = ar1[:, 10240:12288]
    t_dense = ar1[:, 12288:14336]
    t_hq = ar1[:, 14336:16384].bitcast(I32)
    t_ub = ar1[:, 16384:16896]
    t_nib = ar1[:, 18432:19456]

    def dequant_tile(words_ap, nwords, scales_ap, store_fn, sctag):
        nb = 4 * nwords
        ne = 8 * nwords
        nib = t_nib[:, 0:ne]
        lb, hb = nib[:, 0:nb], nib[:, nb:ne]
        hq = t_hq[:, 0:nb]
        ubi = t_hq[:, 512:512 + nb]
        sgn, m4, t4, t6, vv = (t_sgn[:, 0:ne], t_m4[:, 0:ne], t_t4[:, 0:ne],
                               t_t6[:, 0:ne], t_vv[:, 0:ne])
        dense = t_dense[:, 0:ne]
        ub = t_ub[:, 0:nb]
        OP("vector", lambda e: e.tensor_copy(ubi[:, :], words_ap.bitcast(U8)),
           r=["dq_w"], w=["dq_ub"], tag="dq_ub")
        OP("vector", lambda e: e.tensor_scalar(hq[:, :], ubi[:, :], 4, None,
                                               ALU.logical_shift_right),
           r=["dq_ub"], w=["dq_hq"], tag="dq_hq")
        OP("vector", lambda e: e.tensor_copy(hb[:, :], hq[:, :]),
           r=["dq_hq"], w=["dq_nib"], tag="dq_hb")
        OP("vector", lambda e: e.tensor_copy(ub[:, :], ubi[:, :]),
           r=["dq_ub"], w=["dq_ubf"], tag="dq_ubf")
        OP("vector", lambda e: e.scalar_tensor_tensor(
            lb[:, :], hb[:, :], -16.0, ub[:, :], ALU.mult, ALU.add),
            r=["dq_nib", "dq_ubf"], w=["dq_nib"], tag="dq_lb")
        OP("scalar", lambda e: e.activation(sgn[:, :], nib[:, :], AF.Sign,
                                            bias=cc(C_SGB), scale=cc(C_SGS)),
           r=["dq_nib", "cst"], w=["dq_sgn"], tag="dq_sgn")
        OP("vector", lambda e: e.scalar_tensor_tensor(
            m4[:, :], sgn[:, :], 4.0, nib[:, :], ALU.mult, ALU.add),
            r=["dq_sgn", "dq_nib"], w=["dq_m4"], tag="dq_m4")
        OP("scalar", lambda e: e.activation(t4[:, :], m4[:, :], AF.Relu,
                                            bias=cc(C_M8), scale=cc(C_ONE)),
           r=["dq_m4", "cst"], w=["dq_t4"], tag="dq_t4")
        OP("scalar", lambda e: e.activation(t6[:, :], m4[:, :], AF.Relu,
                                            bias=cc(C_M10), scale=cc(C_ONE)),
           r=["dq_m4", "cst"], w=["dq_t6"], tag="dq_t6")
        OP("vector", lambda e: e.scalar_tensor_tensor(
            vv[:, :], m4[:, :], 4.0, t4[:, :], ALU.subtract, ALU.add),
            r=["dq_m4", "dq_t4"], w=["dq_vv"], tag="dq_w1")
        OP("vector", lambda e: e.scalar_tensor_tensor(
            t4[:, :], vv[:, :], 0.5, t6[:, :], ALU.mult, ALU.add),
            r=["dq_vv", "dq_t6"], w=["dq_t4"], tag="dq_v")
        OP("vector", lambda e: e.tensor_tensor(vv[:, :], t4[:, :], sgn[:, :],
                                               ALU.mult),
           r=["dq_t4", "dq_sgn"], w=["dq_vv"], tag="dq_val")
        ng = ne // 128
        v3 = vv.rearrange("p (hh c) -> p hh c", hh=2)
        d3 = dense.rearrange("p (hh c) -> p hh c", hh=2)
        for g in range(ng):
            OP("vector", lambda e, g=g: e.tensor_scalar(
                d3[:, :, 64 * g:64 * (g + 1)], v3[:, :, 64 * g:64 * (g + 1)],
                scales_ap[:, g:g + 1], None, ALU.mult),
                r=["dq_vv", sctag], w=["dq_dense"], lazy=(g < ng - 1), tag="dq_sc")
        store_fn(dense)

    sc_gu = ssc[:, 0:16]
    sc_guh = ssc[:, 16:24].bitcast(F16)[:, 0:16]
    for c in range(2 * NI // 128):
        wsl = spk[:, 0:256] if c % 2 == 0 else spk[:, 256:512]
        OP("sync", lambda e, c=c, wsl=wsl: e.dma_start(
            out=wsl[:, :], in_=gup_d[128 * c:128 * c + 128, :]),
            w=["dq_w"], dma=f"gw{c % 2}", tag="gu_ld")
        OP("sync", lambda e, c=c: e.dma_start(
            out=sc_guh[:, :], in_=gus_d[:, 128 * c:128 * c + 128],
            transpose=True), w=["dq_sch"], dma="gs0", tag="gu_scld")
        OP("vector", lambda e: e.tensor_copy(sc_gu[:, :], sc_guh[:, :]),
           r=["dq_sch"], w=["dq_sc_f"], small=True, tag="gu_scvt")
        for half in range(2):
            wv = wsl[:, 128 * half:128 * (half + 1)]

            def store_gu(dense, c=c, half=half):
                t_stage(dense, "dq_dense", 8 * half, 8, c % 4, "gu")
                if c % 4 == 3 and half == 1:
                    t_flush(sguP_d, 22, c // 4, "gu")

            dequant_tile(wv, 128, sc_gu[:, 8 * half:8 * half + 8],
                         store_gu, "dq_sc_f")

    sc_dn = ssc[:, 32:76]
    sc_dnh = ssc[:, 80:104].bitcast(F16)
    for c in range(E // 128):
        wsl = spk[:, 0:704] if c % 2 == 0 else spk[:, 1024:1728]
        OP("sync", lambda e, c=c, wsl=wsl: e.dma_start(
            out=wsl[:, :], in_=dnp_d[128 * c:128 * c + 128, :]),
            w=["dq_w"], dma=f"gw{c % 2}", tag="dn_ld")
        for tt in range(3):
            OP("sync", lambda e, c=c, tt=tt: e.dma_start(
                out=sc_dnh[:, 16 * tt:16 * tt + 16],
                in_=dns_d[16 * tt:16 * tt + 16, 128 * c:128 * c + 128],
                transpose=True), w=["dq_sch"], dma="gs0", tag="dn_scld")
        OP("vector", lambda e: e.tensor_copy(sc_dn[:, :], sc_dnh[:, 0:44]),
           r=["dq_sch"], w=["dq_sc_f"], small=True, tag="dn_scvt")
        for bq in range(6):
            nwords = 128 if bq < 5 else 64
            wv = wsl[:, 128 * bq:128 * bq + nwords]

            def store_dn(dense, c=c, bq=bq, nwords=nwords):
                for ecl in range(nwords // 16):
                    icg = 8 * bq + ecl
                    sd = kf0 if icg < 32 else kf1
                    scol = 256 * (icg % 32) + 128 * (c % 2)
                    k = 6 + (icg % 2)
                    OP("tensor", lambda e, ecl=ecl, k=k, dense=dense:
                       e.transpose(pbh[k][:, 0:128],
                                   dense[:, 128 * ecl:128 * (ecl + 1)],
                                   ident[:]),
                       r=["dq_dense", "ident"], w=[f"ps{k}"], tag="dnT")
                    OP("scalar", lambda e, sd=sd, scol=scol, k=k:
                       e.copy(sd[:, scol:scol + 128], pbh[k][:, 0:128]),
                       r=[f"ps{k}"], w=[f"std{icg}"], tag="dnTev")
                if c % 2 == 1:
                    for icg in range(8 * bq, 8 * bq + nwords // 16):
                        sd = kf0 if icg < 32 else kf1
                        rb = ((icg // 4) * 4 + (c - 1) // 4) * 128
                        co = 512 * (icg % 4) + 128 * ((c - 1) % 4)
                        OP("gpsimd", lambda e, rb=rb, co=co, icg=icg, sd=sd:
                           e.dma_start(
                               out=sdnP_d[rb:rb + 128, co:co + 256],
                               in_=sd[:, 256 * (icg % 32):256 * (icg % 32) + 256]),
                           r=[f"std{icg}"], dma="fl1", tag="dn_st")

            dequant_tile(wv, nwords, sc_dn[:, 8 * bq:8 * bq + nwords // 16],
                         store_dn, "dq_sc_f")

    sch.barrier()

    wcount = [0]

    def stream_half(srcP, nf, kq, colidx, j):
        i = wcount[0] % 4
        wcount[0] += 1
        half = wb[:, 1024 * i:1024 * i + 1024]
        rb = (kq * nf + colidx) * 128
        eng = "sync" if i % 2 == 0 else "gpsimd"
        OP(eng, lambda e, half=half, rb=rb, j=j, srcP=srcP: e.dma_start(
            out=half, in_=srcP[rb:rb + 128, 1024 * j:1024 * j + 1024]),
            w=[f"wq{i}"], dma=f"wq{i}", tag="wstream")
        return half, f"wq{i}"

    WBALL = ["wq0", "wq1", "wq2", "wq3"]

    def transpose_into_xfm(r, src_bf):
        x5 = src_bf.rearrange("p (b J t h) -> p b h J t", b=2, J=128, t=4, h=2)
        for ec in range(NEC):
            b_, hh, q = ec // 8, (ec % 8) // 4, ec % 4
            k = 6 + (ec % 2)
            OP("tensor", lambda e, b_=b_, hh=hh, q=q, k=k, x5=x5:
               e.transpose(pbh[k][:, 0:128],
                           x5[:, b_, hh, 32 * q:32 * (q + 1), :], ident[:]),
               r=["xbf", "ident"], w=[f"ps{k}"], tag="xT")
            OP("scalar", lambda e, ec=ec, r=r, k=k:
               e.copy(xfm[:, TB * ec + 128 * r:TB * ec + 128 * r + 128],
                      pbh[k][:, 0:128]),
               r=[f"ps{k}"], w=["xfm"], tag="xTev")

    kf = (kf0, kf1)
    vt = (vt0, vt1)

    for blk in range(nblk):
        par = blk % 2
        kfC, kfP = kf[par], kf[1 - par]
        vtC, vtP = vt[par], vt[1 - par]
        kfCt, kfPt = f"kf{par}", f"kf{1 - par}"
        vtCt, vtPt = f"vt{par}", f"vt{1 - par}"
        tb0 = blk * TB

        sch.claim("vector", nop, ["h"] + AR1_ATTN, tag="claimA")

        for r in range(NTT):
            Br = B[:, E * r:E * (r + 1)]
            OP("scalar", lambda e, r=r, Br=Br, tb0=tb0: e.dma_start(
                out=Br[:, :], in_=x_d[tb0 + 128 * r:tb0 + 128 * r + 128, :]),
                w=[f"B{r}"], dma=f"xl{r}", tag="xload")
            OP("vector", lambda e, Br=Br: e.tensor_copy(xbf[:], Br[:, :]),
               r=[f"B{r}"], w=["xbf"], tag="xcast")
            transpose_into_xfm(r, xbf)

        if dbg_stop == "p1" and blk == nblk - 1:
            break
        for fg in range(12):
            bks = (0, 1, 2, 3) if fg % 2 == 0 else (4, 5, 6, 7)
            for kq in range(4):
                for j in range(2):
                    qslot, stag = stream_half(sqP_d, 12, kq, fg, j)
                    for kcl in range(2 * j, 2 * j + 2):
                        kc = 4 * kq + kcl
                        slot = qslot[:, 512 * (kcl - 2 * j):512 * (kcl - 2 * j) + 512]
                        for m in range(4):
                            first, last = kc == 0, kc == NEC - 1
                            bk = bks[m]
                            if fg < 8:
                                OP("tensor", lambda e, slot=slot, kc=kc, m=m, bk=bk,
                                   first=first, last=last: e.matmul(
                                       pb[bk][:, :], slot[:, 128 * m:128 * (m + 1)],
                                       xfm[:, TB * kc:TB * (kc + 1)],
                                       start=first, stop=last),
                                   r=[stag, "xfm"], w=[f"ps{bk}"], lazy=(m < 3),
                                   tag="qk_mm")
                            else:
                                OP("tensor", lambda e, slot=slot, kc=kc, m=m, bk=bk,
                                   first=first, last=last: e.matmul(
                                       pb[bk][:, :],
                                       xfm[:, TB * kc + 128 * m:TB * kc + 128 * (m + 1)],
                                       slot[:, :], start=first, stop=last),
                                   r=[stag, "xfm"], w=[f"ps{bk}"], lazy=(m < 3),
                                   tag="v_mm")
            for m in range(4):
                bk = bks[m]
                if fg < 4:
                    hch = 4 * fg + m
                    OP("scalar", lambda e, hch=hch, bk=bk:
                       e.copy(qf[:, TB * hch:TB * (hch + 1)], pb[bk][:, :]),
                       r=[f"ps{bk}"], w=["qf"], tag="q_ev")
                elif fg < 8:
                    hch = 4 * (fg - 4) + m
                    OP("scalar", lambda e, hch=hch, bk=bk, kfC=kfC:
                       e.copy(kfC[:, TB * hch:TB * (hch + 1)], pb[bk][:, :]),
                       r=[f"ps{bk}"], w=[kfCt], tag="k_ev")
                else:
                    vg = fg - 8
                    OP("scalar", lambda e, vg=vg, m=m, bk=bk, vtC=vtC:
                       e.copy(vtC[:, E * m + 512 * vg:E * m + 512 * (vg + 1)],
                              pb[bk][:, :]),
                       r=[f"ps{bk}"], w=[vtCt], tag="v_ev")

        if dbg_stop == "p2" and blk == nblk - 1:
            break
        c_lo = 4 if blk == 0 else 0
        for hh in range(NH):
            pvb = 4 + (hh % 2)
            for half in range(c_lo // 4, 2):
                for cl in range(4):
                    c = 4 * half + cl
                    if c < 4:
                        ksrc, ktag = kfP, kfPt
                        kcol = TB * hh + 128 * c
                    else:
                        ksrc, ktag = kfC, kfCt
                        kcol = TB * hh + 128 * (c - 4)
                    OP("tensor", lambda e, ksrc=ksrc, kcol=kcol, cl=cl, hh=hh:
                       e.matmul(pb[cl][:, :], ksrc[:, kcol:kcol + 128],
                                qf[:, TB * hh:TB * (hh + 1)], start=True,
                                stop=True),
                       r=[ktag, "qf"], w=[f"ps{cl}"], lazy=(cl < 3),
                       tag="score")
                OP("scalar", lambda e: e.activation(
                    es[:, :], pq[:, :], AF.Exp, bias=cc(C_ZERO),
                    scale=cc(C_EXPSC)),
                    r=["ps0", "ps1", "ps2", "ps3", "cst"],
                    w=["esA", "es0", "es1", "es2", "es3"], tag="exp")
                for cl in range(4):
                    c = 4 * half + cl
                    if c < 4:
                        msrc, mt = mkp, "mkp"
                        mcol = 512 - 128 * c
                        vsrc, vtag = vtP, vtPt
                        vcol = E * c + 128 * hh
                    else:
                        msrc, mt = mko, "mko"
                        mcol = 512 - 128 * (c - 4)
                        vsrc, vtag = vtC, vtCt
                        vcol = E * (c - 4) + 128 * hh
                    OP("vector", lambda e, cl=cl, msrc=msrc, mcol=mcol:
                       e.tensor_tensor(es[:, 512 * cl:512 * cl + 512],
                                       es[:, 512 * cl:512 * cl + 512],
                                       msrc[:, mcol:mcol + 512], ALU.mult),
                       r=["esA", mt], w=[f"es{cl}"], tag="mask")
                    OP("tensor", lambda e, cl=cl, c=c, c_lo=c_lo: e.matmul(
                        pb[6][0:1, :], onescol[:, 0:1],
                        es[:, 512 * cl:512 * cl + 512],
                        start=(c == c_lo), stop=(c == 7)),
                        r=[f"es{cl}", "ones"], w=["ps6"], lazy=(c < 7),
                        tag="summ")
                    OP("tensor", lambda e, vsrc=vsrc, vcol=vcol, cl=cl, c=c,
                       c_lo=c_lo, pvb=pvb:
                       e.matmul(pb[pvb][:, :], vsrc[:, vcol:vcol + 128],
                                es[:, 512 * cl:512 * cl + 512],
                                start=(c == c_lo), stop=(c == 7)),
                       r=[vtag, f"es{cl}"], w=[f"ps{pvb}"], lazy=(c < 7),
                       tag="pv")
            OP("scalar", lambda e: e.copy(srow[:, :], pb[6][0:1, :]),
               r=["ps6"], w=["srow"], small=True, tag="sum_ev")
            OP("vector", lambda e: e.tensor_copy(rh[:, :], srow[:, :]),
               r=["srow"], w=["rhi"], small=True, tag="rhi")
            OP("vector", lambda e: e.scalar_tensor_tensor(
                rl[:, :], rh[:, :], -1.0, srow[:, :], ALU.mult, ALU.add),
                r=["rhi", "srow"], w=["rlo"], small=True, tag="rlo")
            OP("tensor", lambda e: e.matmul(pb[7][:, :], onesrow[:, :],
                                            rh[:, :], start=True, stop=False),
               r=["rhi", "ones"], w=["ps7"], lazy=True, tag="bch")
            OP("tensor", lambda e: e.matmul(pb[7][:, :], onesrow[:, :],
                                            rl[:, :], start=False, stop=True),
               r=["rlo", "ones"], w=["ps7"], tag="bcl")
            OP("vector", lambda e: e.reciprocal(recb[:, :], pb[7][:, :]),
               r=["ps7"], w=["rec"], tag="recip")
            OP("vector", lambda e, hh=hh, pvb=pvb: e.scalar_tensor_tensor(
                af[:, TB * hh:TB * (hh + 1)], pb[pvb][:, :], 1.0, recb[:, :],
                ALU.bypass, ALU.mult),
                r=[f"ps{pvb}", "rec"], w=["af"], tag="attn_ev")

        if dbg_stop == "p3" and blk == nblk - 1:
            break
        for og in range(4):
            bks = (0, 1, 2, 3) if og % 2 == 0 else (4, 5, 6, 7)
            for kq in range(4):
                for j in range(2):
                    qslot, stag = stream_half(soP_d, 4, kq, og, j)
                    for kcl in range(2 * j, 2 * j + 2):
                        kc = 4 * kq + kcl
                        slot = qslot[:, 512 * (kcl - 2 * j):512 * (kcl - 2 * j) + 512]
                        for m in range(4):
                            bk = bks[m]
                            OP("tensor", lambda e, slot=slot, kc=kc, m=m, bk=bk:
                               e.matmul(pb[bk][:, :],
                                        af[:, TB * kc + 128 * m:TB * kc + 128 * (m + 1)],
                                        slot[:, :], start=(kc == 0),
                                        stop=(kc == NEC - 1)),
                               r=[stag, "af"], w=[f"ps{bk}"], lazy=(m < 3),
                               tag="pj_mm")
            for r in range(4):
                bk = bks[r]
                OP("vector", lambda e, r=r, og=og, bk=bk: e.scalar_tensor_tensor(
                    A[:, E * r + 512 * og:E * r + 512 * (og + 1)], pb[bk][:, :],
                    1.0, B[:, E * r + 512 * og:E * r + 512 * (og + 1)],
                    ALU.bypass, ALU.add),
                    r=[f"ps{bk}", f"B{r}"], w=[f"A{r}"], tag="resid1")
        if dbg_stop == "p4a" and blk == nblk - 1:
            break
        for r in range(NTT):
            Ar = A[:, E * r:E * (r + 1)]
            Br = B[:, E * r:E * (r + 1)]
            sc = 6 * r

            def s1(i, r=r, sc=sc):
                return stats[:, sc + i:sc + i + 1], f"st{i}_{r}"
            OP("vector", lambda e, Ar=Ar, sc=sc: e.reduce_sum(
                stats[:, sc:sc + 1], Ar[:, :], axis=AXX),
               r=[f"A{r}"], w=[s1(0)[1]], small=True, tag="l1sum")
            OP("vector", lambda e, sc=sc: e.tensor_scalar(
                stats[:, sc + 1:sc + 2], stats[:, sc:sc + 1],
                1.0 / E, None, ALU.mult),
               r=[s1(0)[1]], w=[s1(1)[1]], small=True, tag="l1mean")
            OP("vector", lambda e, Ar=Ar, Br=Br, sc=sc: e.tensor_scalar(
                Br[:, :], Ar[:, :], stats[:, sc + 1:sc + 2], None, ALU.subtract),
                r=[f"A{r}", s1(1)[1]], w=[f"B{r}"], tag="l1ctr")
            OP("scalar", lambda e, Br=Br, sc=sc: e.activation(
                xbf[:], Br[:, :], AF.Square, bias=cc(C_ZERO), scale=cc(C_ONE),
                accum_out=stats[:, sc + 2:sc + 3]),
                r=[f"B{r}", "cst"], w=[s1(2)[1], "xbf"], tag="l1sq")
            OP("vector", lambda e, sc=sc: e.tensor_scalar(
                stats[:, sc + 3:sc + 4], stats[:, sc + 2:sc + 3], 1.0 / E,
                LN_EPS, ALU.mult, ALU.add),
               r=[s1(2)[1]], w=[s1(3)[1]], small=True, tag="l1var")
            OP("scalar", lambda e, sc=sc: e.activation(
                stats[:, sc + 4:sc + 5], stats[:, sc + 3:sc + 4], AF.Sqrt,
                bias=cc(C_ZERO), scale=cc(C_ONE)),
               r=[s1(3)[1], "cst"], w=[s1(4)[1]], small=True, tag="l1std")
            OP("vector", lambda e, sc=sc: e.reciprocal(
                stats[:, sc + 5:sc + 6], stats[:, sc + 4:sc + 5]),
               r=[s1(4)[1]], w=[s1(5)[1]], small=True, tag="l1rstd")
            OP("vector", lambda e, Ar=Ar, Br=Br, sc=sc: e.tensor_scalar(
                Ar[:, :], Br[:, :], stats[:, sc + 5:sc + 6], None, ALU.mult),
                r=[f"B{r}", s1(5)[1]], w=[f"A{r}"], tag="l1app")
            OP("vector", lambda e, Ar=Ar: e.tensor_copy(xbf[:], Ar[:, :]),
               r=[f"A{r}"], w=["xbf"], tag="h1cast")
            transpose_into_xfm(r, xbf)

        if dbg_stop == "p5" and blk == nblk - 1:
            break
        sch.claim("vector", nop, AR1_ATTN + ["h"], tag="claimH")

        for g2 in range(NI // 512):
            for side in range(2):
                bks = (0, 1, 2, 3) if side == 0 else (4, 5, 6, 7)
                for kq in range(4):
                    for j in range(2):
                        qslot, stag = stream_half(sguP_d, 22, kq,
                                                  11 * side + g2, j)
                        for kcl in range(2 * j, 2 * j + 2):
                            kc = 4 * kq + kcl
                            slot = qslot[:, 512 * (kcl - 2 * j):512 * (kcl - 2 * j) + 512]
                            for m in range(4):
                                OP("tensor", lambda e, slot=slot, kc=kc, m=m,
                                   bks=bks: e.matmul(
                                       pb[bks[m]][:, :],
                                       slot[:, 128 * m:128 * (m + 1)],
                                       xfm[:, TB * kc:TB * (kc + 1)],
                                       start=(kc == 0), stop=(kc == NEC - 1)),
                                   r=[stag, "xfm"], w=[f"ps{bks[m]}"],
                                   lazy=(m < 3), tag="gu_mm")
            for m in range(4):
                ic = 4 * g2 + m
                OP("scalar", lambda e, m=m: e.activation(
                    gs[:], pb[m][:, :], AF.Silu, bias=cc(C_ZERO),
                    scale=cc(C_ONE)),
                    r=[f"ps{m}", "cst"], w=["gs"], tag="silu")
                OP("vector", lambda e, ic=ic, m=m: e.tensor_tensor(
                    h_fm[:, TB * ic:TB * (ic + 1)], gs[:], pb[4 + m][:, :],
                    ALU.mult), r=["gs", f"ps{4 + m}"], w=["h"], tag="silmul")

        if dbg_stop == "p6" and blk == nblk - 1:
            break
        for og in range(4):
            bks = (0, 1, 2, 3) if og % 2 == 0 else (4, 5, 6, 7)
            for icq in range(NIC // 4):
                for j in range(2):
                    qslot, stag = stream_half(sdnP_d, 4, icq, og, j)
                    for icl in range(2 * j, 2 * j + 2):
                        ic = 4 * icq + icl
                        slot = qslot[:, 512 * (icl - 2 * j):512 * (icl - 2 * j) + 512]
                        for r in range(4):
                            bk = bks[r]
                            OP("tensor", lambda e, slot=slot, ic=ic, r=r, bk=bk:
                               e.matmul(pb[bk][:, :],
                                        h_fm[:, TB * ic + 128 * r:TB * ic + 128 * (r + 1)],
                                        slot[:, :], start=(ic == 0),
                                        stop=(ic == NIC - 1)),
                               r=[stag, "h"], w=[f"ps{bk}"], lazy=(r < 3),
                               tag="dn_mm")
            for r in range(4):
                bk = bks[r]
                OP("scalar", lambda e, r=r, og=og, bk=bk: e.copy(
                    B[:, E * r + 512 * og:E * r + 512 * (og + 1)], pb[bk][:, :]),
                    r=[f"ps{bk}"], w=[f"B{r}"], tag="y_ev")
        if dbg_stop == "p7a" and blk == nblk - 1:
            break
        for r in range(NTT):
            Ar = A[:, E * r:E * (r + 1)]
            Br = B[:, E * r:E * (r + 1)]
            Wr = wbF[:, 0:E]
            dum = ar1[:, 2048 * r:2048 * (r + 1)]
            sc = 24 + 10 * r

            def sl(i, sc=sc):
                return stats[:, sc + i:sc + i + 1]

            def sn(i, r=r):
                return f"su{i}_{r}"
            OP("scalar", lambda e, Br=Br, dum=dum, sc=sc: e.activation(
                dum, Br[:, :], AF.Square, bias=cc(C_ZERO), scale=cc(C_ONE),
                accum_out=sl(0, sc)),
                r=[f"B{r}", "cst"], w=[sn(0), "h"], tag="rsq")
            OP("vector", lambda e, sc=sc: e.tensor_scalar(
                sl(1, sc), sl(0, sc), 1.0 / E, RMS_EPS, ALU.mult,
                ALU.add), r=[sn(0)], w=[sn(1)], small=True, tag="rms")
            OP("scalar", lambda e, sc=sc: e.activation(
                sl(2, sc), sl(1, sc), AF.Sqrt, bias=cc(C_ZERO),
                scale=cc(C_ONE)), r=[sn(1), "cst"], w=[sn(2)], small=True, tag="rsqt")
            OP("vector", lambda e, sc=sc: e.reciprocal(sl(3, sc), sl(2, sc)),
               r=[sn(2)], w=[sn(3)], small=True, tag="rinv")
            OP("vector", lambda e, Br=Br, Ar=Ar, Wr=Wr, sc=sc:
               e.scalar_tensor_tensor(
                   Wr[:, :], Br[:, :], sl(3, sc), Ar[:, :], ALU.mult, ALU.add),
               r=[f"B{r}", sn(3), f"A{r}"], w=["h"], tag="resid2")
            OP("vector", lambda e, Wr=Wr, sc=sc: e.reduce_sum(
                sl(4, sc), Wr[:, :], axis=AXX),
               r=["h"], w=[sn(4)], small=True, tag="l2sum")
            OP("vector", lambda e, sc=sc: e.tensor_scalar(
                sl(5, sc), sl(4, sc), 1.0 / E, None, ALU.mult),
               r=[sn(4)], w=[sn(5)], small=True, tag="l2mean")
            OP("vector", lambda e, Ar=Ar, Wr=Wr, sc=sc: e.tensor_scalar(
                Ar[:, :], Wr[:, :], sl(5, sc), None, ALU.subtract),
                r=["h", sn(5)], w=[f"A{r}"], tag="l2ctr")
            OP("scalar", lambda e, Ar=Ar, dum=dum, sc=sc: e.activation(
                dum, Ar[:, :], AF.Square, bias=cc(C_ZERO), scale=cc(C_ONE),
                accum_out=sl(6, sc)),
                r=[f"A{r}", "cst"], w=[sn(6), "h"], tag="l2sq")
            OP("vector", lambda e, sc=sc: e.tensor_scalar(
                sl(7, sc), sl(6, sc), 1.0 / E, LN_EPS, ALU.mult,
                ALU.add), r=[sn(6)], w=[sn(7)], small=True, tag="l2var")
            OP("scalar", lambda e, sc=sc: e.activation(
                sl(8, sc), sl(7, sc), AF.Sqrt, bias=cc(C_ZERO),
                scale=cc(C_ONE)), r=[sn(7), "cst"], w=[sn(8)], small=True, tag="l2std")
            OP("vector", lambda e, sc=sc: e.reciprocal(sl(9, sc), sl(8, sc)),
               r=[sn(8)], w=[sn(9)], small=True, tag="l2rstd")
            OP("vector", lambda e, Ar=Ar, Br=Br, sc=sc: e.tensor_scalar(
                Br[:, :], Ar[:, :], sl(9, sc), None, ALU.mult),
                r=[f"A{r}", sn(9)], w=[f"B{r}"], tag="l2app")
            OP("scalar", lambda e, r=r, Br=Br, tb0=tb0: e.dma_start(
                out=y_d[tb0 + 128 * r:tb0 + 128 * r + 128, :], in_=Br[:, :]),
                r=[f"B{r}"], dma=f"ys{r}", tag="ystore")

    sch.barrier()
    tapmap = dict(A=A, B=B, xfm=xfm, qf=qf, es=es, af=af, kf0=kf0, kf1=kf1,
                  vt0=vt0, vt1=vt1, h=ar1, wb=wb, xbf=xbf, stats=stats,
                  sdn=sdnP_d, sgu=sguP_d, sq=sqP_d)
    for (name, ap_out) in tap_outs:
        key = name.split("__")[1]
        sap = tapmap[key]
        n_r = ap_out.shape[0]
        OP("gpsimd", lambda e, ap_out=ap_out, sap=sap, n_r=n_r: e.dma_start(
            out=ap_out, in_=sap[0:n_r, :]), r=[], dma="tap", tag="tap")

    sch.barrier()
    print(f"[build] ops={len(sch.ops)} instrs~={sch.n_instr} "
          f"sbuf_free={nc.sbuf_bytes_remaining} psum_free={nc.psum_banks_remaining}")
    blk_cm = nc.Block()
    block = blk_cm.__enter__()
    sch.emit(block)
    blk_cm.__exit__(None, None, None)
    st.close()
    return nc


def _consts_np():
    ident = np.eye(128, dtype=ml_dtypes.bfloat16)
    onesc = np.ones((128, 1), dtype=ml_dtypes.bfloat16)
    onesr = np.ones((1, 128), dtype=ml_dtypes.bfloat16)
    r = np.arange(128)[:, None]
    u = np.arange(1024)[None, :]
    MP = (r >= (u - 512)).astype(ml_dtypes.bfloat16)
    MO = (r <= (u - 512)).astype(ml_dtypes.bfloat16)
    mask = np.concatenate([MP, MO], axis=1)
    cst = np.tile(np.array(CONSTS, dtype=np.float32), (128, 1))
    return ident, onesc, onesr, mask, cst


def _pi(n):
    out = np.zeros(n, dtype=np.int64)
    b0 = 0
    while b0 < n:
        w = min(1024, n - b0)
        e = np.arange(w)
        j, kn = e // 8, e % 8
        t, hbit = kn // 2, kn % 2
        ep = (w // 2) * hbit + 4 * j + t
        out[b0 + ep] = b0 + e
        b0 += w
    return out


def make_in_map(xb, wqkv, wout, gup, gus, dnp, dns):
    ident, onesc, onesr, mask, cst = _consts_np()
    pi_e = _pi(E)
    pi_i = _pi(NI)
    rows = np.concatenate([pi_i, NI + pi_i])
    return dict(x=np.ascontiguousarray(xb),
                wqkv=np.ascontiguousarray(np.asarray(wqkv)[:, pi_e]),
                wout=np.asarray(wout),
                gup=np.ascontiguousarray(np.asarray(gup)[rows]),
                gus=np.ascontiguousarray(np.asarray(gus)[:, rows]),
                dnp=np.asarray(dnp),
                dns=np.pad(np.asarray(dns), ((0, 4), (0, 0))),
                identin=ident, onescin=onesc,
                onesrin=onesr, maskin=mask, cstin=cst)


def kernel(L0_attn_x, L0_attn_w_qkv, L0_attn_w_out, L0_ffn_gate_up_packed,
           L0_ffn_gate_up_scales, L0_ffn_down_packed, L0_ffn_down_scales):
    nc = build_program(nblk=8)
    in_maps = [make_in_map(L0_attn_x[b], L0_attn_w_qkv, L0_attn_w_out,
                           L0_ffn_gate_up_packed, L0_ffn_gate_up_scales,
                           L0_ffn_down_packed, L0_ffn_down_scales)
               for b in range(8)]
    res = run_bass_kernel_spmd(nc, in_maps, core_ids=list(range(8)))
    return np.stack([np.asarray(r["y"], dtype=np.float32)
                     for r in res.results])
